# version 10
# baseline (speedup 1.0000x reference)
import numpy as np
from contextlib import ExitStack

import concourse.bass as bass
import concourse.mybir as mybir
from concourse.bass_utils import run_bass_kernel_spmd

F32 = mybir.dt.float32
BF16 = mybir.dt.bfloat16
U32 = mybir.dt.uint32
Alu = mybir.AluOpType
AF = mybir.ActivationFunctionType
AX = mybir.AxisListType

T = 2048
D = 1024
NH = 16
NI = 2816
NT = 16
NC2 = 8
NIC = NI // 128
ALPHA = float(32.0 ** -0.25)
BETA = float(2.0 ** 0.5)
SQ_BIAS = float(1.0 / 2.0 ** 0.5)
MAGIC = 12582912.0
QMAX = 127.0
EPS_LN = 1e-5
EPS_ATT = 1e-12

STAGE = 3
DBG = ()
CUT = None


def build_nc(dbg=(), stage=STAGE, cut=None):
    nc = bass.Bass("TRN2", target_bir_lowering=False)

    x_h = nc.dram_tensor("x", [T, D], F32, kind="ExternalInput").ap()
    wq_h = nc.dram_tensor("wq", [D, 256], F32, kind="ExternalInput").ap()
    wk_h = nc.dram_tensor("wk", [D, 256], F32, kind="ExternalInput").ap()
    wv_h = nc.dram_tensor("wv", [D, D], F32, kind="ExternalInput").ap()
    wo_h = nc.dram_tensor("wo", [D, D], F32, kind="ExternalInput").ap()
    wg_h = nc.dram_tensor("wg", [NI, D], F32, kind="ExternalInput").ap()
    wu_h = nc.dram_tensor("wu", [NI, D], F32, kind="ExternalInput").ap()
    wd_h = nc.dram_tensor("wd", [D, NI], F32, kind="ExternalInput").ap()
    cvec_h = nc.dram_tensor("cvec", [8, D], F32, kind="ExternalInput").ap()
    wmod_h = nc.dram_tensor("wmod", [128, 6 * D], F32, kind="ExternalInput").ap()
    clipx_h = nc.dram_tensor("clipx", [1, 1], F32, kind="ExternalInput").ap()
    clipd_h = nc.dram_tensor("clipd", [1, 1], F32, kind="ExternalInput").ap()
    out_h = nc.dram_tensor("out", [T, D], F32, kind="ExternalOutput").ap()

    dbg_h = {}
    dbg_shapes = {"q": ([256, T], BF16), "k": ([256, T], BF16),
                  "v": ([T, NH * 64], BF16),
                  "x2": ([T, D], F32), "mod": ([1, 6 * 1024], F32),
                  "xh": ([T, D], BF16), "ln": ([128, 96], F32),
                  "xt": ([T, D], F32), "zz": ([128, 512], F32), "dd": ([128, 64], F32),
                  "yr": ([64, 8 * 1024], BF16), "qh": ([16, 4 * T], BF16),
                  "am": ([128, 512], BF16), "uq": ([128, 512], BF16),
                  "ys": ([128, 512], BF16),
                  "stf": ([128, 520], BF16), "st1": ([1, 260], BF16),
                  "st2": ([16, 260], BF16), "qfa": ([128, 256], BF16),
                  "qfb": ([128, 256], BF16)}
    for name in dbg:
        shp, dt = dbg_shapes[name]
        dbg_h[name] = nc.dram_tensor("dbg_" + name, shp, dt,
                                     kind="ExternalOutput").ap()

    cc_mod_in = nc.dram_tensor("cc_mod_in", [8, 6 * D], F32, kind="Internal").ap()
    cc_mod_out = nc.dram_tensor("cc_mod_out", [8, 6 * D], F32, kind="Internal").ap()
    cc_m1_in = nc.dram_tensor("cc_m1_in", [1, 128], F32, kind="Internal").ap()
    cc_m1_out = nc.dram_tensor("cc_m1_out", [1, 128], F32, kind="Internal").ap()
    cc_m2_in = nc.dram_tensor("cc_m2_in", [1, 128], F32, kind="Internal").ap()
    cc_m2_out = nc.dram_tensor("cc_m2_out", [1, 128], F32, kind="Internal").ap()
    scr_xhat = nc.dram_tensor("scr_xhat", [T, D], BF16, kind="Internal").ap()
    scr_xq = nc.dram_tensor("scr_xq", [T, D], BF16, kind="Internal").ap()
    scr_wg = nc.dram_tensor("scr_wg", [NI, D], BF16, kind="Internal").ap()
    scr_wu = nc.dram_tensor("scr_wu", [NI, D], BF16, kind="Internal").ap()
    scr_wd = nc.dram_tensor("scr_wd", [D, NI], BF16, kind="Internal").ap()
    scr_g = nc.dram_tensor("scr_g", [NI, T], F32, kind="Internal").ap()
    scr_x2 = nc.dram_tensor("scr_x2", [T, D], F32, kind="Internal").ap()

    es = ExitStack()
    with es:
        def sb(name, shape, dtype=F32):
            return es.enter_context(nc.sbuf_tensor(name, shape, dtype))

        live = {}

        def sb_open(name, shape, dtype=F32):
            cm = nc.sbuf_tensor(name, shape, dtype)
            t = cm.__enter__()
            live[name] = cm
            return t

        def sb_close(*names):
            for n in names:
                live.pop(n).__exit__(None, None, None)

        PIDT = sb("PIDT", [1, 1], U32)
        ONES1 = sb("ONES1", [1, 128])
        ONESB = sb("ONESB", [128, 1], BF16)
        ONESBR = sb("ONESBR", [1, 512], BF16)
        ZROWB = sb("ZROWB", [1, 128], BF16)
        IDENT = sb("IDENT", [128, 128])
        MSTT = sb("MSTT", [128, 512], BF16)
        ESTRA = sb("ESTRA", [16, 128], BF16)
        ESTRB = sb("ESTRB", [16, 128], BF16)
        ETILE = sb("ETILE", [16, 128], BF16)
        MODF = sb("MODF", [128, 8, 2])
        ROWTMP = sb("ROWTMP", [1, 512])
        CB1B = sb("CB1B", [128, D])
        SBB = sb("SBB", [128, D])
        GAB = sb("GAB", [128, D])
        GBB = sb("GBB", [128, D])
        SCL = sb("SCL", [128, 8])
        SCR = sb("SCR", [1, 32])
        STATM = sb("STATM", [128, NT])
        STATR = sb("STATR", [128, NT])
        STTMP = sb("STTMP", [128, NT])
        BNS = sb("BNS", [128, 2 * NT])
        GMX = sb("GMX", [128, 32])
        CCS = sb("CCS", [1, 128])
        WSC = sb("WSC", [128, 64])
        EVS = sb("EVS", [128, 64])
        SQB = sb("SQB", [128, 1])

        XT = sb_open("XT", [128, NT, D])

        XHB = sb_open("XHB", [128, NT, D], BF16)
        WMODS = sb_open("WMODS", [128, 6 * D])
        MODP = sb_open("MODP", [8, 6 * D])
        CV = sb_open("CV", [8, D])
        CVS = sb_open("CVS", [8, D])
        CVT = sb_open("CVT", [128, 8])
        MODR = sb_open("MODR", [1, 6 * D])
        TIN = sb_open("TIN", [8, 128])

        P = [es.enter_context(nc.psum_tensor(f"P{i}", [128, 512], F32, bank_id=i))
             for i in range(8)]

        semh, cnt = {}, {}
        for s in ["ld", "gd", "te", "dv", "ac", "st", "cc"]:
            semh[s] = es.enter_context(nc.semaphore(s))
            cnt[s] = 0

        def inc(instr, s, n=None):
            if n is None:
                n = 16 if s in ("ld", "gd", "st") else 1
            cnt[s] += n
            instr.then_inc(semh[s], n)
            return instr

        def wt(eng, *sems):
            for s in sems:
                if cnt[s] > 0:
                    eng.wait_ge(semh[s], cnt[s])

        def wat(eng, s, v):
            if v > 0:
                eng.wait_ge(semh[s], v)

        def vf(v):
            inc(v.engine_nop(), "dv")
            wat(v, "dv", cnt["dv"])

        block = es.enter_context(nc.Block())

        def finish():
            @block.sync
            def _(sync):
                wt(sync, "st", "dv", "ac", "te", "gd", "ld", "cc")
            for n in reversed(list(live.keys())):
                sb_close(n)

        @block.sync
        def _(sync):
            inc(sync.dma_start(out=XT[:], in_=x_h.rearrange("(c p) d -> p c d", p=128)), "ld")
            inc(sync.dma_start(out=PIDT[:], in_=nc.partition_id_tensor.ap()), "ld")
            inc(sync.dma_start(out=SCR[0:1, 0:1], in_=clipx_h), "ld")
            inc(sync.dma_start(out=SCR[0:1, 1:2], in_=clipd_h), "ld")
            inc(sync.dma_start(out=CV[:], in_=cvec_h), "ld")
        ld_in = cnt["ld"]

        @block.gpsimd
        def _(g):
            inc(g.dma_start(out=WMODS[:], in_=wmod_h), "gd")
        gd_wmod = cnt["gd"]

        @block.vector
        def _(v):
            v.memset(ONES1[:], 1.0)
            v.memset(ONESB[:], 1.0)
            v.memset(ONESBR[:], 1.0)
            v.memset(ZROWB[:], 0.0)
            v.memset(IDENT[:], 1.0)
            v.memset(MSTT[:], 1.0)
            v.memset(ESTRA[:], 1.0)
            v.memset(ESTRB[:], 1.0)
            v.memset(ETILE[:], 1.0)
            v.memset(SQB[:], SQ_BIAS)
            inc(v.engine_nop(), "dv")
        dv_const = cnt["dv"]

        @block.gpsimd
        def _(g):
            wat(g, "dv", dv_const)
            g.affine_select(out=IDENT[:], in_=IDENT[:], pattern=[[-1, 128]],
                            compare_op=Alu.is_ge, fill=0.0, base=0, channel_multiplier=1)
            g.affine_select(out=IDENT[:], in_=IDENT[:], pattern=[[1, 128]],
                            compare_op=Alu.is_ge, fill=0.0, base=0, channel_multiplier=-1)
            g.affine_select(out=MSTT[:], in_=MSTT[:], pattern=[[-128, 2], [1, 256]],
                            compare_op=Alu.is_ge, fill=0.0, base=0, channel_multiplier=-1)
            g.affine_select(out=ESTRA[:], in_=ESTRA[:], pattern=[[-1, 8], [0, 16]],
                            compare_op=Alu.is_ge, fill=0.0, base=0, channel_multiplier=1)
            g.affine_select(out=ESTRA[:], in_=ESTRA[:], pattern=[[1, 8], [0, 16]],
                            compare_op=Alu.is_ge, fill=0.0, base=0, channel_multiplier=-1)
            g.affine_select(out=ESTRB[:], in_=ESTRB[:], pattern=[[-1, 8], [0, 16]],
                            compare_op=Alu.is_ge, fill=0.0, base=-8, channel_multiplier=1)
            g.affine_select(out=ESTRB[:], in_=ESTRB[:], pattern=[[1, 8], [0, 16]],
                            compare_op=Alu.is_ge, fill=0.0, base=8, channel_multiplier=-1)
            g.affine_select(out=ETILE[:], in_=ETILE[:], pattern=[[0, 8], [-1, 16]],
                            compare_op=Alu.is_ge, fill=0.0, base=0, channel_multiplier=1)
            g.affine_select(out=ETILE[:], in_=ETILE[:], pattern=[[0, 8], [1, 16]],
                            compare_op=Alu.is_ge, fill=0.0, base=0, channel_multiplier=-1)
            inc(g.engine_nop(), "gd", 1)
        gd_masks = cnt["gd"]
        if cut == "const":
            finish()
            return nc

        @block.scalar
        def _(s):
            wat(s, "ld", ld_in)
            inc(s.activation(out=CVS[:], in_=CV[:], func=AF.Silu), "ac")
        ac_cvs = cnt["ac"]

        @block.gpsimd
        def _(g):
            wat(g, "ac", ac_cvs)
            with g.register("pid") as pid:
                g.reg_load(pid, PIDT[0:1, 0:1])
                for k in range(8):
                    with g.If_eq(pid, k):
                        g.dma_start(out=TIN[0:8, 0:128],
                                    in_=CVS[:, 128 * k:128 * (k + 1)]).then_inc(semh["gd"], 16)
            cnt["gd"] += 16
        gd_cvt = cnt["gd"]

        @block.tensor
        def _(t):
            wat(t, "gd", gd_masks)
            wat(t, "gd", gd_cvt)
            inc(t.transpose(P[0].ap()[0:128, 0:8], in_=TIN[0:8, 0:128],
                            identity=IDENT[0:8, 0:8]), "te")
        te_cvt = cnt["te"]

        @block.vector
        def _(v):
            wat(v, "te", te_cvt)
            inc(v.tensor_copy(out=CVT[:], in_=P[0].ap()[0:128, 0:8]), "dv")
        dv_cvt = cnt["dv"]

        mm_done = []
        @block.tensor
        def _(t):
            wat(t, "dv", dv_cvt)
            wat(t, "gd", gd_wmod)
            for j in range(12):
                bank = P[1 + (j % 4)]
                if j >= 4:
                    wat(t, "dv", dv_cvt + (j - 4) + 1)
                inc(t.matmul(bank.ap()[0:8, 0:512], lhsT=CVT[:],
                             rhs=WMODS[:, 512 * j:512 * (j + 1)], start=True, stop=True), "te")
                mm_done.append(cnt["te"])

        @block.vector
        def _(v):
            for j in range(12):
                wat(v, "te", mm_done[j])
                inc(v.tensor_copy(out=MODP[:, 512 * j:512 * (j + 1)],
                                  in_=P[1 + (j % 4)].ap()[0:8, 0:512]), "dv")
        dv_modp = cnt["dv"]

        @block.sync
        def _(sync):
            wat(sync, "dv", dv_modp)
            inc(sync.dma_start(out=cc_mod_in, in_=MODP[:]), "st")
        st_ccm = cnt["st"]

        @block.gpsimd
        def _(g):
            wat(g, "st", st_ccm)
            inc(g.collective_compute("AllReduce", Alu.add,
                                     replica_groups=[list(range(8))],
                                     ins=[cc_mod_in.opt()], outs=[cc_mod_out.opt()]), "cc", 1)
            wat(g, "cc", cnt["cc"])
            with g.register("pid2") as pid:
                g.reg_load(pid, PIDT[0:1, 0:1])
                for k in range(8):
                    with g.If_eq(pid, k):
                        g.dma_start(out=MODR[0:1, 0:6 * D],
                                    in_=cc_mod_out[k:k + 1, :]).then_inc(semh["gd"], 16)
            cnt["gd"] += 16
        gd_modr = cnt["gd"]

        @block.vector
        def _(v):
            wat(v, "gd", gd_modr)
            sa = MODR[0:1, 0:D]
            ca = MODR[0:1, D:2 * D]
            ga = MODR[0:1, 2 * D:3 * D]
            sbr = MODR[0:1, 3 * D:4 * D]
            cb = MODR[0:1, 4 * D:5 * D]
            gb = MODR[0:1, 5 * D:6 * D]
            t1 = MODP[0:1, 0:D]
            t2 = MODP[0:1, D:2 * D]
            v.tensor_scalar(out=ca, in0=ca, scalar1=1.0, scalar2=None, op0=Alu.add)
            v.tensor_scalar(out=cb, in0=cb, scalar1=1.0, scalar2=None, op0=Alu.add)
            for r in (ga, gb):
                v.tensor_scalar(out=t1, in0=r, scalar1=0.0, scalar2=2.0,
                                op0=Alu.is_ge, op1=Alu.mult)
                v.tensor_scalar(out=t1, in0=t1, scalar1=1.0, scalar2=None, op0=Alu.subtract)
                v.tensor_scalar(out=t2, in0=r, scalar1=-1.0, scalar2=None, op0=Alu.mult)
                v.tensor_tensor(out=t2, in0=t2, in1=r, op=Alu.max)
                v.tensor_scalar(out=t2, in0=t2, scalar1=0.25, scalar2=None, op0=Alu.max)
                v.tensor_mul(out=r, in0=t1, in1=t2)
            v.reciprocal(out=t1, in_=ca)
            v.tensor_mul(out=sa, in0=sa, in1=t1)
            inc(v.engine_nop(), "dv")
        dv_modv = cnt["dv"]

        if "mod" in dbg_h:
            @block.sync
            def _(sync):
                wat(sync, "dv", dv_modv)
                inc(sync.dma_start(out=dbg_h["mod"], in_=MODR[0:1, 0:6 * D]), "st")

        if cut == "mod":
            finish()
            return nc

        prev = dv_modv
        for j, coloff in ((0, 0), (1, D)):
            for c in range(8):
                @block.tensor
                def _(t, j=j, coloff=coloff, c=c):
                    wat(t, "dv", prev)
                    inc(t.transpose(P[5].ap()[0:128, 0:1],
                                    in_=MODR[0:1, coloff + 128 * c:coloff + 128 * (c + 1)],
                                    identity=IDENT[0:1, 0:1]), "te")
                te_x = cnt["te"]

                @block.vector
                def _(v, j=j, c=c, te_x=te_x):
                    wat(v, "te", te_x)
                    inc(v.tensor_copy(out=MODF[:, c, j:j + 1], in_=P[5].ap()[0:128, 0:1]), "dv")
                prev = cnt["dv"]
        dv_modf = cnt["dv"]

        prev_bc = dv_modv
        for dstt, coloff in ((CB1B, 4 * D), (SBB, 3 * D), (GAB, 2 * D), (GBB, 5 * D)):
            for jj in range(2):
                @block.sync
                def _(sync, coloff=coloff, jj=jj):
                    wat(sync, "dv", prev_bc)
                    wat(sync, "te", cnt["te"])
                    inc(sync.dma_start(out=ROWTMP[:],
                                       in_=MODR[0:1, coloff + 512 * jj:coloff + 512 * (jj + 1)]), "st")
                st_r = cnt["st"]

                @block.tensor
                def _(t, jj=jj, st_r=st_r):
                    wat(t, "st", st_r)
                    wat(t, "dv", prev_bc)
                    inc(t.matmul(P[6].ap()[0:128, 0:512], lhsT=ONES1[0:1, 0:128],
                                 rhs=ROWTMP[0:1, 0:512],
                                 start=True, stop=True), "te")
                te_x = cnt["te"]

                @block.vector
                def _(v, dstt=dstt, jj=jj, te_x=te_x):
                    wat(v, "te", te_x)
                    inc(v.tensor_copy(out=dstt[:, 512 * jj:512 * (jj + 1)],
                                      in_=P[6].ap()[0:128, 0:512]), "dv")
                prev_bc = cnt["dv"]
        dv_bcasts = cnt["dv"]
        if cut == "modf":
            finish()
            return nc

        @block.vector
        def _(v):
            wat(v, "ld", ld_in)
            for c in range(NT):
                v.tensor_scalar(out=XHB[:, c, :], in0=XT[:, c, :], scalar1=1.0,
                                scalar2=0.0, op0=Alu.mult, op1=Alu.add,
                                accum_out=STATM[:, c:c + 1])
                v.scalar_tensor_tensor(out=XHB[:, c, :], in0=XT[:, c, :], scalar=1.0,
                                       in1=XT[:, c, :], op0=Alu.mult, op1=Alu.mult,
                                       accum_out=STTMP[:, c:c + 1])
            vf(v)
            v.tensor_scalar(out=STATM[:], in0=STATM[:], scalar1=1.0 / D, scalar2=None,
                            op0=Alu.mult)
            v.tensor_scalar(out=STTMP[:], in0=STTMP[:], scalar1=1.0 / D, scalar2=EPS_LN,
                            op0=Alu.mult, op1=Alu.add)
            vf(v)
            v.tensor_mul(out=BNS[:, 0:NT], in0=STATM[:], in1=STATM[:])
            vf(v)
            v.tensor_tensor(out=STTMP[:], in0=STTMP[:], in1=BNS[:, 0:NT],
                            op=Alu.subtract)
            vf(v)
            v.reciprocal(out=STATR[:], in_=STTMP[:])
            if "ln" in dbg_h:
                vf(v)
                v.tensor_copy(out=BNS[:, 16:32], in_=STATR[:])
            inc(v.engine_nop(), "dv")
        dv_bn1 = cnt["dv"]

        @block.scalar
        def _(s):
            wat(s, "dv", dv_bn1)
            inc(s.activation(out=STATR[:], in_=STATR[:], func=AF.Sqrt), "ac")
        ac_sq1 = cnt["ac"]

        st_lndbg = 0
        if "ln" in dbg_h:
            @block.sync
            def _(sync):
                wat(sync, "ac", ac_sq1)
                inc(sync.dma_start(out=dbg_h["ln"][:, 48:64], in_=BNS[:, 16:32]), "st")
                inc(sync.dma_start(out=dbg_h["ln"][:, 64:80], in_=STATR[:]), "st")
                wt(sync, "st")
            st_lndbg = cnt["st"]

        @block.vector
        def _(v):
            wat(v, "ac", ac_sq1)
            wat(v, "st", st_lndbg)
            v.tensor_mul(out=BNS[:, 0:NT], in0=STATR[:], in1=STATR[:])
            vf(v)
            v.tensor_mul(out=BNS[:, 0:NT], in0=BNS[:, 0:NT], in1=STTMP[:])
            vf(v)
            v.tensor_scalar(out=BNS[:, 0:NT], in0=BNS[:, 0:NT], scalar1=-0.5,
                            scalar2=1.5, op0=Alu.mult, op1=Alu.add)
            vf(v)
            v.tensor_mul(out=STATR[:], in0=STATR[:], in1=BNS[:, 0:NT])
            vf(v)
            for c in range(NT):
                v.tensor_scalar(out=XHB[:, c, :], in0=XT[:, c, :],
                                scalar1=STATM[:, c:c + 1], scalar2=STATR[:, c:c + 1],
                                op0=Alu.subtract, op1=Alu.mult)
            inc(v.engine_nop(), "dv")
        dv_xhat = cnt["dv"]

        @block.sync
        def _(sync):
            wat(sync, "dv", dv_xhat)
            inc(sync.dma_start(out=scr_xhat.rearrange("(c p) d -> p c d", p=128), in_=XHB[:]), "st")
            wt(sync, "st")
        st_xhat = cnt["st"]
        if "xt" in dbg_h:
            @block.sync
            def _(sync):
                wat(sync, "ld", ld_in)
                inc(sync.dma_start(out=dbg_h["xt"].rearrange("(c p) d -> p c d", p=128),
                                   in_=XT[:]), "st")
        if "ln" in dbg_h:
            @block.sync
            def _(sync):
                wat(sync, "dv", dv_xhat)
                inc(sync.dma_start(out=dbg_h["ln"][:, 0:16], in_=STATM[:]), "st")
                inc(sync.dma_start(out=dbg_h["ln"][:, 16:32], in_=STATR[:]), "st")
                inc(sync.dma_start(out=dbg_h["ln"][:, 32:48], in_=STTMP[:]), "st")
        if "xh" in dbg_h:
            @block.gpsimd
            def _(g):
                wat(g, "st", st_xhat)
                inc(g.dma_start(out=dbg_h["xh"], in_=scr_xhat), "gd")
        if cut == "ln":
            finish()
            return nc

        sb_close("TIN", "MODR", "CVT", "CVS", "CV", "MODP", "WMODS", "XHB")

        QF = sb_open("QF", [128, 2, T], BF16)
        KF = sb_open("KF", [128, 2, T], BF16)
        KTM = sb_open("KTM", [128, NT, 256], BF16)
        VTM = sb_open("VTM", [128, NT, NH * 64], BF16)
        XHT = sb_open("XHT", [128, 8, T], BF16)
        WPA = sb_open("WPA", [128, 8, 1536], BF16)
        WTMP = sb_open("WTMP", [128, 512])
        WSA = sb_open("WSA", [1, 1536], BF16)
        SACOL = sb_open("SACOL", [128, 8], BF16)

        @block.sync
        def _(sync):
            for dch in range(8):
                inc(sync.dma_start(out=XHT[:, dch, :],
                                   in_=scr_xhat[:, 128 * dch:128 * (dch + 1)], transpose=True), "ld")
        ld_xht = cnt["ld"]

        prev_fold = dv_xhat
        for dch in range(8):
            @block.gpsimd
            def _(g, dch=dch):
                wat(g, "dv", prev_fold)
                wat(g, "st", st_xhat)
                inc(g.dma_start(out=WTMP[:, 0:256], in_=wq_h[128 * dch:128 * (dch + 1), :]), "gd")
                inc(g.dma_start(out=WTMP[:, 256:512], in_=wk_h[128 * dch:128 * (dch + 1), :]), "gd")
            gd_w = cnt["gd"]

            @block.vector
            def _(v, dch=dch, gd_w=gd_w):
                wat(v, "gd", gd_w)
                wat(v, "dv", dv_modf)
                v.tensor_scalar(out=WPA[:, dch, 0:512], in0=WTMP[:, 0:512],
                                scalar1=MODF[:, dch, 1:2], scalar2=ALPHA,
                                op0=Alu.mult, op1=Alu.mult)
                inc(v.engine_nop(), "dv")
            dv_qk = cnt["dv"]

            prev_v = dv_qk
            for vh in range(2):
                @block.gpsimd
                def _(g, dch=dch, vh=vh, prev_v=prev_v):
                    wat(g, "dv", prev_v)
                    inc(g.dma_start(out=WTMP[:, 0:512],
                                    in_=wv_h[128 * dch:128 * (dch + 1),
                                             512 * vh:512 * (vh + 1)]), "gd")
                gd_v = cnt["gd"]

                @block.vector
                def _(v, dch=dch, vh=vh, gd_v=gd_v):
                    wat(v, "gd", gd_v)
                    v.tensor_scalar(out=WPA[:, dch, 512 * (vh + 1):512 * (vh + 2)],
                                    in0=WTMP[:, 0:512],
                                    scalar1=MODF[:, dch, 1:2], scalar2=None, op0=Alu.mult)
                    inc(v.engine_nop(), "dv")
                prev_v = cnt["dv"]
            prev_fold = cnt["dv"]

        @block.vector
        def _(v):
            wat(v, "dv", prev_fold)
            inc(v.tensor_copy(out=SACOL[:], in_=MODF[:, :, 0]), "dv")
        dv_sac = cnt["dv"]

        te_wsa = []
        @block.tensor
        def _(t):
            wat(t, "dv", dv_sac)
            for j in range(3):
                for dch in range(8):
                    inc(t.matmul(P[1 + j].ap()[0:1, 0:512], lhsT=SACOL[:, dch:dch + 1],
                                 rhs=WPA[:, dch, 512 * j:512 * (j + 1)],
                                 start=(dch == 0), stop=(dch == 7)), "te")
                te_wsa.append(cnt["te"])

        @block.vector
        def _(v):
            for j in range(3):
                wat(v, "te", te_wsa[j])
                inc(v.tensor_copy(out=WSA[0:1, 512 * j:512 * (j + 1)],
                                  in_=P[1 + j].ap()[0:1, 0:512]), "dv")
        dv_wsa = cnt["dv"]
        if cut == "wprep":
            finish()
            return nc

        bank_evac = [0] * 8

        def fm_proj(dst, col0):
            for fch in range(2):
                for j in range(4):
                    b = fch * 4 + j
                    @block.tensor
                    def _(t, fch=fch, j=j, b=b):
                        wat(t, "dv", dv_wsa)
                        wat(t, "ld", ld_xht)
                        wat(t, "ac", bank_evac[b])
                        bank = P[b]
                        for dch in range(8):
                            inc(t.matmul(bank.ap()[0:128, 0:512],
                                         lhsT=WPA[:, dch, col0 + 128 * fch:col0 + 128 * (fch + 1)],
                                         rhs=XHT[:, dch, 512 * j:512 * (j + 1)],
                                         start=(dch == 0), stop=False), "te")
                        inc(t.matmul(bank.ap()[0:128, 0:512],
                                     lhsT=WSA[0:1, col0 + 128 * fch:col0 + 128 * (fch + 1)],
                                     rhs=ONESBR[0:1, 0:512], start=False, stop=True), "te")
                    fill_te = cnt["te"]

                    @block.scalar
                    def _(s, fch=fch, j=j, b=b, fill_te=fill_te):
                        wat(s, "te", fill_te)
                        inc(s.activation(out=dst[:, fch, 512 * j:512 * (j + 1)],
                                         in_=P[b].ap()[0:128, 0:512], func=AF.Copy), "ac")
                    bank_evac[b] = cnt["ac"]
            return cnt["ac"]

        ac_q = fm_proj(QF, 0)
        ac_k = fm_proj(KF, 256)

        ac_tmv = []
        for c in range(NT):
            @block.tensor
            def _(t, c=c):
                if c == 0:
                    wat(t, "ac", bank_evac[2])
                elif c == 1:
                    wat(t, "ac", bank_evac[5])
                if c >= 2:
                    wat(t, "ac", ac_tmv[c - 2])
                for dch in range(8):
                    inc(t.matmul(P[(c % 2) * 3 + 0].ap()[0:128, 0:256],
                                 lhsT=XHT[:, dch, 128 * c:128 * (c + 1)],
                                 rhs=WPA[:, dch, 256:512],
                                 start=(dch == 0), stop=(dch == 7)), "te")
                for half in range(2):
                    for dch in range(8):
                        inc(t.matmul(P[(c % 2) * 3 + 1 + half].ap()[0:128, 0:512],
                                     lhsT=XHT[:, dch, 128 * c:128 * (c + 1)],
                                     rhs=WPA[:, dch, 512 + 512 * half:1024 + 512 * half],
                                     start=(dch == 0), stop=(dch == 7)), "te")
            te_tm = cnt["te"]

            @block.scalar
            def _(s, c=c, te_tm=te_tm):
                wat(s, "te", te_tm)
                inc(s.activation(out=KTM[:, c, :], in_=P[(c % 2) * 3 + 0].ap()[0:128, 0:256],
                                 func=AF.Copy), "ac")
                for half in range(2):
                    inc(s.activation(out=VTM[:, c, 512 * half:512 * (half + 1)],
                                     in_=P[(c % 2) * 3 + 1 + half].ap()[0:128, 0:512],
                                     func=AF.Copy), "ac")
            ac_tmv.append(cnt["ac"])
        ac_tm = cnt["ac"]

        if "q" in dbg_h:
            @block.gpsimd
            def _(g):
                wat(g, "ac", ac_q)
                inc(g.dma_start(out=dbg_h["q"].rearrange("(a p) t -> p a t", p=128), in_=QF[:]), "gd")
        if "k" in dbg_h:
            @block.gpsimd
            def _(g):
                wat(g, "ac", ac_k)
                inc(g.dma_start(out=dbg_h["k"].rearrange("(a p) t -> p a t", p=128), in_=KF[:]), "gd")
        if "v" in dbg_h:
            @block.gpsimd
            def _(g):
                wat(g, "ac", ac_tm)
                inc(g.dma_start(out=dbg_h["v"].rearrange("(c p) f -> p c f", p=128), in_=VTM[:]), "gd")

        if cut == "proj":
            finish()
            return nc

        sb_close("SACOL", "WSA", "WTMP", "WPA", "XHT")

        WOB = sb_open("WOB", [128, 8, D], BF16)
        WOTMP = sb_open("WOTMP", [128, D])

        prev_wo = dv_bcasts
        for ch in range(8):
            @block.gpsimd
            def _(g, ch=ch):
                wat(g, "dv", prev_wo)
                wat(g, "ac", ac_tm)
                inc(g.dma_start(out=WOTMP[:], in_=wo_h[128 * ch:128 * (ch + 1), :]), "gd")
            gd_wo = cnt["gd"]

            @block.vector
            def _(v, ch=ch, gd_wo=gd_wo):
                wat(v, "gd", gd_wo)
                inc(v.tensor_mul(out=WOB[:, ch, :], in0=WOTMP[:], in1=GAB[:]), "dv")
            prev_wo = cnt["dv"]
        dv_wob = cnt["dv"]

        @block.vector
        def _(v):
            wat(v, "dv", dv_wob)
            inc(v.engine_nop(), "dv")
        sb_close("WOTMP")

        if stage >= 2:
            QHG = sb_open("QHG", [16, 4, T], BF16)
            KHG = sb_open("KHG", [16, 4, T], BF16)
            KFA = sb_open("KFA", [128, 2, 512], BF16)
            KFB = sb_open("KFB", [128, 2, 512], BF16)
            KFL = sb_open("KFL", [128, 2, 64], BF16)
            QFA = sb_open("QFA", [128, 256], BF16)
            QFB = sb_open("QFB", [128, 256], BF16)
            UQ = sb_open("UQ", [128, 512], BF16)
            AM = sb_open("AM", [128, 512], BF16)
            SBF = sb_open("SBF", [128, 4 * 130], BF16)
            SC1B = sb_open("SC1B", [1, 4 * 65], BF16)
            SC2B = sb_open("SC2B", [16, 4 * 65], BF16)
            YRAW = sb_open("YRAW", [64, 4 * 256], BF16)
            YS2 = sb_open("YS2", [128, 2, 256], BF16)
            ZTM = sb_open("ZTM", [128, 64])
            ZFM = sb_open("ZFM", [64, 128])
            QE = sb_open("QE", [128, 256])
            SEL = sb_open("SEL", [64, 512])

            @block.vector
            def _(v):
                wat(v, "dv", dv_wob)
                inc(v.memset(SEL[:], 1.0), "dv")
            dv_sel = cnt["dv"]

            @block.gpsimd
            def _(g):
                wat(g, "dv", dv_sel)
                g.affine_select(out=SEL[:], in_=SEL[:],
                                pattern=[[-16, 2], [-32, 2], [-8, 2], [0, 64]],
                                compare_op=Alu.is_ge, fill=0.0, base=0,
                                channel_multiplier=1)
                g.affine_select(out=SEL[:], in_=SEL[:],
                                pattern=[[16, 2], [32, 2], [8, 2], [0, 64]],
                                compare_op=Alu.is_ge, fill=0.0, base=0,
                                channel_multiplier=-1)
                inc(g.engine_nop(), "gd", 1)
            gd_sel = cnt["gd"]

            @block.tensor
            def _(t):
                wat(t, "gd", gd_sel)

            L = {"u_ac": 0, "am_dv": 0, "qf_dv": 0, "y_ac": 0, "sbf_ac": 0,
                 "den_te": 0, "kf_te": 0, "z_done": 0, "ys_dv": dv_wob,
                 "p7_dv": dv_bcasts, "yraw_busy": 0, "hg_st": 0, "kf_dv": 0}

            for g4 in range(4):
                h0 = 4 * g4

                @block.tensor
                def _(t):
                    wat(t, "ac", cnt["ac"])
                    for bk in range(3):
                        inc(t.matmul(P[bk].ap()[0:128, 0:512], lhsT=ZROWB[0:1, 0:128],
                                     rhs=ONESBR[0:1, 0:512], start=True, stop=True), "te")

                @block.sync
                def _(sync):
                    wat(sync, "ac", ac_tm)
                    wat(sync, "dv", dv_wob)
                    wat(sync, "te", L["kf_te"])
                    for hl in range(4):
                        h = h0 + hl
                        fch, off = divmod(16 * h, 128)
                        inc(sync.dma_start(out=QHG[:, hl, :], in_=QF[off:off + 16, fch, :]), "st")
                        inc(sync.dma_start(out=KHG[:, hl, :], in_=KF[off:off + 16, fch, :]), "st")
                L["hg_st"] = cnt["st"]

                if g4 == 0 and "qh" in dbg_h:
                    @block.sync
                    def _(sync):
                        wat(sync, "st", L["hg_st"])
                        inc(sync.dma_start(out=dbg_h["qh"].rearrange("p (a t) -> p a t", a=4),
                                           in_=QHG[:]), "st")
                    L["hg_st"] = cnt["st"]

                for c in range(NC2):
                    @block.vector
                    def _(v):
                        wat(v, "ac", ac_tm)
                        wat(v, "te", L["kf_te"])
                        for kk in range(2):
                            tc = 2 * c + kk
                            base = KTM[:, tc, :]
                            for half, dst in ((0, KFA), (1, KFB)):
                                in0 = bass.AP(base.tensor, base.offset + 16 * h0 + 8 * half,
                                              [base.ap[0], [16, 4], [1, 8], [0, 16]])
                                in1 = bass.AP(base.tensor, base.offset + 16 * h0,
                                              [base.ap[0], [16, 4], [0, 8], [1, 16]])
                                v.tensor_tensor(
                                    out=dst[:, kk, :].rearrange("p (a b e) -> p a b e", a=4, b=8),
                                    in0=in0, in1=in1, op=Alu.mult)
                            v.tensor_scalar(out=KFL[:, kk, :],
                                            in0=base[:, 16 * h0:16 * h0 + 64],
                                            scalar1=BETA, scalar2=None, op0=Alu.mult)
                        inc(v.engine_nop(), "dv")
                    L["kf_dv"] = cnt["dv"]

                    for hl in range(4):
                        h = h0 + hl
                        q_sl = QHG[:, hl, 256 * c:256 * (c + 1)]

                        @block.tensor
                        def _(t):
                            wat(t, "ac", L["u_ac"])
                            wat(t, "st", L["hg_st"])
                            for kk in range(2):
                                inc(t.matmul(P[3].ap()[0:128, 256 * kk:256 * (kk + 1)],
                                             lhsT=KHG[:, hl, 256 * c + 128 * kk:
                                                      256 * c + 128 * (kk + 1)],
                                             rhs=q_sl, start=(kk == 0), stop=(kk == 1)), "te")
                        te_stil = cnt["te"]

                        @block.scalar
                        def _(s):
                            wat(s, "te", te_stil)
                            wat(s, "dv", L["am_dv"])
                            inc(s.activation(out=UQ[:], in_=P[3].ap()[0:128, 0:512],
                                             func=AF.Square, bias=SQB[:, 0:1]), "ac")
                        L["u_ac"] = cnt["ac"]

                        @block.vector
                        def _(v):
                            wat(v, "ac", L["u_ac"])
                            wat(v, "te", L["den_te"])
                            inc(v.scalar_tensor_tensor(out=AM[:], in0=UQ[:], scalar=0.5,
                                                       in1=MSTT[:], op0=Alu.add, op1=Alu.mult), "dv")
                        L["am_dv"] = cnt["dv"]

                        if g4 == 0 and c == 0 and hl == 0 and "am" in dbg_h:
                            @block.sync
                            def _(sync):
                                wat(sync, "dv", L["am_dv"])
                                inc(sync.dma_start(out=dbg_h["am"], in_=AM[:]), "st")
                                inc(sync.dma_start(out=dbg_h["uq"], in_=UQ[:]), "st")
                            L["hg_st"] = max(L["hg_st"], cnt["st"])

                        @block.tensor
                        def _(t):
                            wat(t, "ac", L["y_ac"])
                            wat(t, "dv", L["ys_dv"])
                            wat(t, "dv", L["qf_dv"])
                            inc(t.matmul(P[6].ap()[0:128, 0:256], lhsT=ETILE[:], rhs=q_sl,
                                         start=True, stop=True), "te")
                            inc(t.matmul(P[4].ap()[0:128, 0:256], lhsT=ESTRA[:], rhs=q_sl,
                                         start=True, stop=True), "te")
                            inc(t.matmul(P[4].ap()[0:128, 256:512], lhsT=ESTRB[:], rhs=q_sl,
                                         start=True, stop=True), "te")
                        te_qf = cnt["te"]

                        @block.scalar
                        def _(s):
                            wat(s, "te", te_qf)
                            inc(s.activation(out=QE[:], in_=P[6].ap()[0:128, 0:256],
                                             func=AF.Copy), "ac")
                        ac_qe = cnt["ac"]

                        @block.vector
                        def _(v):
                            wat(v, "te", te_qf)
                            wat(v, "ac", ac_qe)
                            inc(v.tensor_tensor(out=QFA[:], in0=P[4].ap()[0:128, 0:256],
                                                in1=QE[:], op=Alu.mult), "dv")
                            inc(v.tensor_tensor(out=QFB[:], in0=P[4].ap()[0:128, 256:512],
                                                in1=QE[:], op=Alu.mult), "dv")
                        L["qf_dv"] = cnt["dv"]

                        if g4 == 0 and c == 1 and hl == 0 and "qfa" in dbg_h:
                            @block.sync
                            def _(sync):
                                wat(sync, "dv", L["qf_dv"])
                                inc(sync.dma_start(out=dbg_h["qfa"], in_=QFA[:]), "st")
                                inc(sync.dma_start(out=dbg_h["qfb"], in_=QFB[:]), "st")
                            L["hg_st"] = max(L["hg_st"], cnt["st"])

                        @block.tensor
                        def _(t):
                            wat(t, "dv", L["qf_dv"])
                            wat(t, "dv", L["am_dv"])
                            if c > 0:
                                wat(t, "ac", L["sbf_ac"])
                            mms = []
                            for kk in range(2):
                                mms.append((VTM[:, 2 * c + kk, 64 * h:64 * (h + 1)],
                                            AM[:, 256 * kk:256 * (kk + 1)]))
                            if c > 0:
                                mms.append((SBF[:, 130 * hl:130 * hl + 64], QFA[:]))
                                mms.append((SBF[:, 130 * hl + 65:130 * hl + 129], QFB[:]))
                                mms.append((SC1B[0:1, 65 * hl:65 * hl + 64], ONESBR[0:1, 0:256]))
                                mms.append((SC2B[:, 65 * hl:65 * hl + 64], q_sl))
                            for i, (lh, rh) in enumerate(mms):
                                inc(t.matmul(P[4].ap()[0:64, 0:256], lhsT=lh, rhs=rh,
                                             start=(i == 0), stop=(i == len(mms) - 1)), "te")
                        te_y = cnt["te"]

                        @block.tensor
                        def _(t):
                            wat(t, "dv", L["z_done"])
                            if c > 0:
                                wat(t, "ac", L["sbf_ac"])
                            mms = []
                            for kk in range(2):
                                for qs in range(2):
                                    dc = 8 * (4 * qs + hl)
                                    mms.append((P[5].ap()[0:128, dc:dc + 1],
                                                AM[:, 256 * kk + 128 * qs:
                                                   256 * kk + 128 * (qs + 1)],
                                                ONESB[:]))
                            if c > 0:
                                for qs in range(2):
                                    q0 = 128 * qs
                                    dc = 8 * (4 * qs + hl)
                                    dstp = P[5].ap()[0:128, dc:dc + 1]
                                    mms.append((dstp, QFA[:, q0:q0 + 128],
                                                SBF[:, 130 * hl + 64:130 * hl + 65]))
                                    mms.append((dstp, QFB[:, q0:q0 + 128],
                                                SBF[:, 130 * hl + 129:130 * hl + 130]))
                                    mms.append((dstp, ONESBR[0:1, q0:q0 + 128],
                                                SC1B[0:1, 65 * hl + 64:65 * hl + 65]))
                                    mms.append((dstp, QHG[:, hl, 256 * c + q0:256 * c + q0 + 128],
                                                SC2B[:, 65 * hl + 64:65 * hl + 65]))
                            if hl == 0:
                                inc(t.matmul(P[5].ap()[0:128, 0:64],
                                             lhsT=ZROWB[0:1, 0:128],
                                             rhs=ONESBR[0:1, 0:64],
                                             start=True, stop=True), "te")
                            for i, (ou, lh, rh) in enumerate(mms):
                                inc(t.matmul(ou, lhsT=lh, rhs=rh,
                                             start=False, stop=True), "te")
                        L["den_te"] = cnt["te"]

                        if g4 == 0 and c == 0 and hl == 3 and "dd" in dbg_h:
                            @block.vector
                            def _(v):
                                wat(v, "te", L["den_te"])
                                inc(v.tensor_copy(out=EVS[:, 0:64],
                                                  in_=P[5].ap()[0:128, 0:64]), "dv")
                            dv_dd = cnt["dv"]

                            @block.sync
                            def _(sync, dv_dd=dv_dd):
                                wat(sync, "dv", dv_dd)
                                inc(sync.dma_start(out=dbg_h["dd"], in_=EVS[:, 0:64]), "st")

                        @block.scalar
                        def _(s):
                            wat(s, "te", te_y)
                            wat(s, "dv", L["yraw_busy"])
                            inc(s.activation(out=YRAW[:, 256 * hl:256 * (hl + 1)],
                                             in_=P[4].ap()[0:64, 0:256], func=AF.Copy), "ac")
                        L["y_ac"] = cnt["ac"]

                        te_state = cnt["te"]
                        if c < NC2 - 1:
                            @block.tensor
                            def _(t):
                                wat(t, "dv", L["kf_dv"])
                                if c > 0:
                                    wat(t, "ac", L["sbf_ac"])
                                for kk in range(2):
                                    fs = False
                                    vsl = VTM[:, 2 * c + kk, 64 * h:64 * (h + 1)]
                                    inc(t.matmul(P[0].ap()[0:128, 72 * hl:72 * hl + 64],
                                                 lhsT=KFA[:, kk, 128 * hl:128 * (hl + 1)],
                                                 rhs=vsl, start=False, stop=True), "te")
                                    inc(t.matmul(P[0].ap()[0:128, 72 * hl + 64:72 * hl + 65],
                                                 lhsT=KFA[:, kk, 128 * hl:128 * (hl + 1)],
                                                 rhs=ONESB[:], start=False, stop=True), "te")
                                    inc(t.matmul(P[1].ap()[0:128, 72 * hl:72 * hl + 64],
                                                 lhsT=KFB[:, kk, 128 * hl:128 * (hl + 1)],
                                                 rhs=vsl, start=False, stop=True), "te")
                                    inc(t.matmul(P[1].ap()[0:128, 72 * hl + 64:72 * hl + 65],
                                                 lhsT=KFB[:, kk, 128 * hl:128 * (hl + 1)],
                                                 rhs=ONESB[:], start=False, stop=True), "te")
                                    inc(t.matmul(P[2].ap()[32:33, 72 * hl:72 * hl + 64],
                                                 lhsT=ONESB[:], rhs=vsl, start=False, stop=True), "te")
                                    inc(t.matmul(P[2].ap()[32:33, 72 * hl + 64:72 * hl + 65],
                                                 lhsT=ONESB[:], rhs=ONESB[:], start=False, stop=True), "te")
                                    inc(t.matmul(P[2].ap()[0:16, 72 * hl:72 * hl + 64],
                                                 lhsT=KFL[:, kk, 16 * hl:16 * (hl + 1)],
                                                 rhs=vsl, start=False, stop=True), "te")
                                    inc(t.matmul(P[2].ap()[0:16, 72 * hl + 64:72 * hl + 65],
                                                 lhsT=KFL[:, kk, 16 * hl:16 * (hl + 1)],
                                                 rhs=ONESB[:], start=False, stop=True), "te")
                            te_state = cnt["te"]

                        if c < NC2 - 1:
                            @block.scalar
                            def _(s):
                                wat(s, "te", te_state)
                                inc(s.activation(out=SBF[:, 130 * hl:130 * hl + 65],
                                                 in_=P[0].ap()[0:128, 72 * hl:72 * hl + 65],
                                                 func=AF.Copy), "ac")
                                inc(s.activation(out=SBF[:, 130 * hl + 65:130 * hl + 130],
                                                 in_=P[1].ap()[0:128, 72 * hl:72 * hl + 65],
                                                 func=AF.Copy), "ac")
                                inc(s.activation(out=SC1B[0:1, 65 * hl:65 * (hl + 1)],
                                                 in_=P[2].ap()[32:33, 72 * hl:72 * hl + 65],
                                                 func=AF.Copy), "ac")
                                inc(s.activation(out=SC2B[:, 65 * hl:65 * (hl + 1)],
                                                 in_=P[2].ap()[0:16, 72 * hl:72 * hl + 65],
                                                 func=AF.Copy), "ac")
                            L["sbf_ac"] = cnt["ac"]

                            if g4 == 0 and c == 0 and hl == 3 and "stf" in dbg_h:
                                @block.sync
                                def _(sync):
                                    wat(sync, "ac", L["sbf_ac"])
                                    inc(sync.dma_start(out=dbg_h["stf"], in_=SBF[:]), "st")
                                    inc(sync.dma_start(out=dbg_h["st1"], in_=SC1B[:]), "st")
                                    inc(sync.dma_start(out=dbg_h["st2"], in_=SC2B[:]), "st")
                    L["kf_te"] = cnt["te"]

                    @block.vector
                    def _(v):
                        wat(v, "te", L["den_te"])
                        wat(v, "st", L.get("zzdump_st", 0))
                        inc(v.tensor_scalar(out=ZTM[:], in0=P[5].ap()[0:128, 0:64],
                                            scalar1=EPS_ATT, scalar2=None, op0=Alu.add), "dv")
                        vf(v)
                        inc(v.reciprocal(out=ZTM[:], in_=ZTM[:]), "dv")
                    dv_z = cnt["dv"]

                    if g4 == 0 and "zz" in dbg_h:
                        @block.sync
                        def _(sync, c=c):
                            wat(sync, "dv", dv_z)
                            inc(sync.dma_start(out=dbg_h["zz"][:, 64 * c:64 * (c + 1)],
                                               in_=ZTM[:]), "st")
                        L["zzdump_st"] = cnt["st"]

                    @block.tensor
                    def _(t):
                        wat(t, "dv", dv_z)
                        inc(t.transpose(P[5].ap()[0:64, 128:256], in_=ZTM[:], identity=IDENT[:]), "te")
                    te_zt = cnt["te"]

                    @block.vector
                    def _(v):
                        wat(v, "te", te_zt)
                        inc(v.tensor_copy(out=ZFM[:], in_=P[5].ap()[0:64, 128:256]), "dv")
                    L["z_done"] = cnt["dv"]

                    @block.tensor
                    def _(t):
                        wat(t, "dv", L["z_done"])
                        wat(t, "dv", L["qf_dv"])
                        for pr in range(2):
                            for qs in range(2):
                                inc(t.matmul(P[6].ap()[0:128, 256 * pr + 128 * qs:
                                                       256 * pr + 128 * (qs + 1)],
                                             lhsT=SEL[:, 256 * pr + 128 * qs:
                                                      256 * pr + 128 * (qs + 1)],
                                             rhs=ZFM[:],
                                             start=True, stop=True), "te")
                    te_zb = cnt["te"]

                    if g4 == 0 and "yr" in dbg_h:
                        @block.sync
                        def _(sync, c=c):
                            wat(sync, "ac", L["y_ac"])
                            inc(sync.dma_start(out=dbg_h["yr"][:, 1024 * c:1024 * (c + 1)],
                                               in_=YRAW[:]), "st")

                    @block.vector
                    def _(v):
                        wat(v, "te", te_zb)
                        wat(v, "ac", L["y_ac"])
                        wat(v, "st", L.get("ysdump_st", 0))
                        for hl in range(4):
                            pr, half = divmod(hl, 2)
                            inc(v.tensor_tensor(out=YS2[64 * half:64 * half + 64, pr, :],
                                                in0=YRAW[:, 256 * hl:256 * (hl + 1)],
                                                in1=P[6].ap()[64 * half:64 * half + 64,
                                                              256 * pr:256 * (pr + 1)],
                                                op=Alu.mult), "dv")
                    L["ys_dv"] = cnt["dv"]
                    L["yraw_busy"] = cnt["dv"]

                    if g4 == 3 and c == 0 and "ys" in dbg_h:
                        @block.sync
                        def _(sync):
                            wat(sync, "dv", L["ys_dv"])
                            inc(sync.dma_start(out=dbg_h["ys"].rearrange("p (a q) -> p a q", a=2),
                                               in_=YS2[:]), "st")
                        L["ysdump_st"] = cnt["st"]

                    for qs in range(2):
                        for dh in range(2):
                            @block.tensor
                            def _(t):
                                wat(t, "dv", L["ys_dv"])
                                wat(t, "dv", L["p7_dv"])
                                for pr in range(2):
                                    inc(t.matmul(P[7].ap()[0:128, 0:512],
                                                 lhsT=YS2[:, pr, 128 * qs:128 * (qs + 1)],
                                                 rhs=WOB[:, 2 * g4 + pr, 512 * dh:512 * (dh + 1)],
                                                 start=(pr == 0), stop=(pr == 1)), "te")
                            te_p7 = cnt["te"]

                            @block.vector
                            def _(v):
                                wat(v, "te", te_p7)
                                tc = 2 * c + qs
                                inc(v.tensor_add(out=XT[:, tc, 512 * dh:512 * (dh + 1)],
                                                 in0=P[7].ap()[0:128, 0:512],
                                                 in1=XT[:, tc, 512 * dh:512 * (dh + 1)]), "dv")
                            L["p7_dv"] = cnt["dv"]

            sb_close("SEL", "QE", "ZFM", "ZTM", "YS2", "YRAW", "SC2B", "SC1B", "SBF", "AM",
                     "UQ", "QFB", "QFA", "KFL", "KFB", "KFA", "KHG", "QHG")

        if "x2" in dbg_h:
            @block.sync
            def _(sync):
                wt(sync, "dv")
                inc(sync.dma_start(out=dbg_h["x2"].rearrange("(c p) d -> p c d", p=128),
                                   in_=XT[:]), "st")

        if stage < 3:
            @block.sync
            def _(sync):
                wt(sync, "dv", "ac", "te", "gd", "ld")
                inc(sync.dma_start(out=out_h.rearrange("(c p) d -> p c d", p=128), in_=XT[:]), "st")
                wt(sync, "st")
            for n in reversed(list(live.keys())):
                sb_close(n)
            return nc

        sb_close("WOB", "VTM", "KTM", "KF", "QF")
        H2C = sb_open("H2C", [128, D])

        @block.vector
        def _(v):
            wt(v, "dv")
            wt(v, "ac")
            wt(v, "te")
            for c in range(NT):
                v.tensor_scalar(out=H2C[:], in0=XT[:, c, :], scalar1=1.0,
                                scalar2=0.0, op0=Alu.mult, op1=Alu.add,
                                accum_out=STATM[:, c:c + 1])
                v.scalar_tensor_tensor(out=H2C[:], in0=XT[:, c, :], scalar=1.0,
                                       in1=XT[:, c, :], op0=Alu.mult, op1=Alu.mult,
                                       accum_out=STTMP[:, c:c + 1])
            vf(v)
            v.tensor_scalar(out=STATM[:], in0=STATM[:], scalar1=1.0 / D, scalar2=None,
                            op0=Alu.mult)
            v.tensor_scalar(out=STTMP[:], in0=STTMP[:], scalar1=1.0 / D, scalar2=EPS_LN,
                            op0=Alu.mult, op1=Alu.add)
            vf(v)
            v.tensor_mul(out=BNS[:, 0:NT], in0=STATM[:], in1=STATM[:])
            vf(v)
            v.tensor_tensor(out=STTMP[:], in0=STTMP[:], in1=BNS[:, 0:NT],
                            op=Alu.subtract)
            vf(v)
            v.reciprocal(out=STATR[:], in_=STTMP[:])
            inc(v.engine_nop(), "dv")
        dv_bn2 = cnt["dv"]

        @block.scalar
        def _(s):
            wat(s, "dv", dv_bn2)
            inc(s.activation(out=STATR[:], in_=STATR[:], func=AF.Sqrt), "ac")
        ac_sq2 = cnt["ac"]

        @block.vector
        def _(v):
            wat(v, "ac", ac_sq2)
            v.tensor_mul(out=BNS[:, 0:NT], in0=STATR[:], in1=STATR[:])
            vf(v)
            v.tensor_mul(out=BNS[:, 0:NT], in0=BNS[:, 0:NT], in1=STTMP[:])
            vf(v)
            v.tensor_scalar(out=BNS[:, 0:NT], in0=BNS[:, 0:NT], scalar1=-0.5,
                            scalar2=1.5, op0=Alu.mult, op1=Alu.add)
            vf(v)
            v.tensor_mul(out=STATR[:], in0=STATR[:], in1=BNS[:, 0:NT])
            vf(v)
            for c in range(NT):
                v.tensor_scalar(out=H2C[:], in0=XT[:, c, :],
                                scalar1=STATM[:, c:c + 1], scalar2=STATR[:, c:c + 1],
                                op0=Alu.subtract, op1=Alu.mult)
                v.tensor_mul(out=H2C[:], in0=H2C[:], in1=CB1B[:])
                v.tensor_add(out=H2C[:], in0=H2C[:], in1=SBB[:])
                v.reduce_max(out=GMX[:, c:c + 1], in_=H2C[:], axis=AX.X,
                             apply_absolute_value=True)
            v.reduce_max(out=GMX[:, 30:31], in_=GMX[:, 0:NT], axis=AX.X)
            inc(v.engine_nop(), "dv")
        dv_h2 = cnt["dv"]

        @block.sync
        def _(sync):
            wat(sync, "dv", dv_h2)
            inc(sync.dma_start(out=scr_x2.rearrange("(c p) d -> p c d", p=128), in_=XT[:]), "st")
            wt(sync, "st")
        st_x2 = cnt["st"]

        sb_close("H2C", "XT")

        GBSC3 = GBSC_tile = None
        GBSC = sb_open("GBSC", [128, D])
        SWDR = sb_open("SWDR", [1, D])
        XRB = sb_open("XRB", [128, 2, D])
        HT2 = sb_open("HT2", [128, D])
        XQT = sb_open("XQT", [128, NT, D], BF16)

        @block.tensor
        def _(t):
            wat(t, "dv", dv_h2)
            inc(t.transpose(P[5].ap()[0:1, 0:128], in_=GMX[:, 30:31], identity=IDENT[:]), "te")
        te_mx = cnt["te"]

        @block.vector
        def _(v):
            wat(v, "te", te_mx)
            inc(v.tensor_copy(out=CCS[:], in_=P[5].ap()[0:1, 0:128]), "dv")
            vf(v)
            inc(v.reduce_max(out=SCR[0:1, 2:3], in_=CCS[:], axis=AX.X), "dv")
            vf(v)
            inc(v.tensor_scalar(out=CCS[:], in0=ONES1[0:1, 0:128], scalar1=SCR[0:1, 2:3],
                                scalar2=None, op0=Alu.mult), "dv")
        dv_mx = cnt["dv"]

        @block.sync
        def _(sync):
            wat(sync, "dv", dv_mx)
            inc(sync.dma_start(out=cc_m1_in, in_=CCS[:]), "st")
        st_m1 = cnt["st"]

        @block.gpsimd
        def _(g):
            wat(g, "st", st_m1)
            inc(g.collective_compute("AllReduce", Alu.max,
                                     replica_groups=[list(range(8))],
                                     ins=[cc_m1_in.opt()], outs=[cc_m1_out.opt()]), "cc", 1)
            wat(g, "cc", cnt["cc"])
            inc(g.dma_start(out=CCS[:], in_=cc_m1_out), "gd")
        gd_m1 = cnt["gd"]

        @block.scalar
        def _(s):
            wat(s, "ld", ld_in)
            inc(s.activation(out=SCR[0:1, 4:5], in_=SCR[0:1, 0:1], func=AF.Sigmoid), "ac")
        ac_sig = cnt["ac"]

        @block.vector
        def _(v):
            wat(v, "gd", gd_m1)
            wat(v, "ac", ac_sig)
            v.scalar_tensor_tensor(out=SCR[0:1, 5:6], in0=SCR[0:1, 4:5],
                                   scalar=1.0 / QMAX, in1=CCS[0:1, 0:1],
                                   op0=Alu.mult, op1=Alu.mult)
            vf(v)
            v.reciprocal(out=SCR[0:1, 6:7], in_=SCR[0:1, 5:6])
            inc(v.engine_nop(), "dv")
        dv_sx = cnt["dv"]

        @block.tensor
        def _(t):
            wat(t, "dv", dv_sx)
            inc(t.matmul(P[5].ap()[0:128, 0:1], lhsT=ONES1[0:1, 0:128], rhs=SCR[0:1, 6:7],
                         start=True, stop=True), "te")
            inc(t.matmul(P[5].ap()[0:128, 8:9], lhsT=ONES1[0:1, 0:128], rhs=SCR[0:1, 5:6],
                         start=True, stop=True), "te")
        te_sx = cnt["te"]

        @block.vector
        def _(v):
            wat(v, "te", te_sx)
            v.tensor_copy(out=SCL[:, 0:1], in_=P[5].ap()[0:128, 0:1])
            inc(v.tensor_copy(out=SCL[:, 1:2], in_=P[5].ap()[0:128, 8:9]), "dv")
        dv_sxb = cnt["dv"]

        prev_q = 0
        for c in range(NT):
            @block.gpsimd
            def _(g, c=c):
                wat(g, "st", st_x2)
                if c >= 2:
                    wat(g, "dv", prev_q)
                inc(g.dma_start(out=XRB[:, c % 2, :],
                                in_=scr_x2[128 * c:128 * (c + 1), :]), "gd")
            gd_xr = cnt["gd"]

            @block.vector
            def _(v, c=c, gd_xr=gd_xr):
                wat(v, "gd", gd_xr)
                wat(v, "dv", dv_sxb)
                v.tensor_scalar(out=HT2[:], in0=XRB[:, c % 2, :],
                                scalar1=STATM[:, c:c + 1], scalar2=STATR[:, c:c + 1],
                                op0=Alu.subtract, op1=Alu.mult)
                v.tensor_mul(out=HT2[:], in0=HT2[:], in1=CB1B[:])
                v.tensor_add(out=HT2[:], in0=HT2[:], in1=SBB[:])
                v.tensor_scalar(out=HT2[:], in0=HT2[:], scalar1=SCL[:, 0:1],
                                scalar2=MAGIC, op0=Alu.mult, op1=Alu.add)
                v.tensor_scalar(out=HT2[:], in0=HT2[:], scalar1=MAGIC,
                                scalar2=-128.0, op0=Alu.subtract, op1=Alu.max)
                inc(v.tensor_scalar(out=XQT[:, c, :], in0=HT2[:], scalar1=127.0,
                                    scalar2=None, op0=Alu.min), "dv")
            prev_q = cnt["dv"]
        dv_xq = cnt["dv"]

        @block.sync
        def _(sync):
            wat(sync, "dv", dv_xq)
            inc(sync.dma_start(out=scr_xq.rearrange("(c p) d -> p c d", p=128), in_=XQT[:]), "st")
            wt(sync, "st")
        st_xq = cnt["st"]

        sb_close("XQT")
        GBUF = sb_open("GBUF", [128, 2, T])
        XQF = sb_open("XQF", [128, 8, T], BF16)
        WQTMP = sb_open("WQTMP", [128, 2, NI])
        WQI = sb_open("WQI", [128, 2, NI], BF16)

        @block.sync
        def _(sync):
            for dch in range(8):
                inc(sync.dma_start(out=XQF[:, dch, :],
                                   in_=scr_xq[:, 128 * dch:128 * (dch + 1)], transpose=True), "ld")
        ld_xqf = cnt["ld"]

        qjobs = []
        for ch in range(NIC):
            qjobs.append((wg_h, scr_wg, 0, ch, D))
        for ch in range(NIC):
            qjobs.append((wu_h, scr_wu, 22, ch, D))
        for ch in range(8):
            qjobs.append((wd_h, scr_wd, 44, ch, NI))

        qv_done = [0, 0]
        qst_done = [0, 0]
        for jj, (w_h, scr, wsc_off, ch, width) in enumerate(qjobs):
            pb = jj % 2
            @block.gpsimd
            def _(g, w_h=w_h, ch=ch, width=width, pb=pb, prev=qv_done[pb]):
                wat(g, "dv", prev)
                wat(g, "st", st_xq)
                inc(g.dma_start(out=WQTMP[:, pb, 0:width],
                                in_=w_h[128 * ch:128 * (ch + 1), :]), "gd")
            gd_l = cnt["gd"]

            @block.vector
            def _(v, wsc_off=wsc_off, ch=ch, width=width, pb=pb, gd_l=gd_l,
                  prevst=qst_done[pb]):
                wat(v, "gd", gd_l)
                wat(v, "st", prevst)
                v.reduce_max(out=WSC[:, wsc_off + ch:wsc_off + ch + 1],
                             in_=WQTMP[:, pb, 0:width], axis=AX.X,
                             apply_absolute_value=True)
                vf(v)
                v.tensor_scalar(out=WSC[:, wsc_off + ch:wsc_off + ch + 1],
                                in0=WSC[:, wsc_off + ch:wsc_off + ch + 1],
                                scalar1=1.0 / QMAX, scalar2=None, op0=Alu.mult)
                vf(v)
                v.reciprocal(out=EVS[:, 0:1],
                             in_=WSC[:, wsc_off + ch:wsc_off + ch + 1])
                vf(v)
                v.tensor_scalar(out=WQTMP[:, pb, 0:width], in0=WQTMP[:, pb, 0:width],
                                scalar1=EVS[:, 0:1], scalar2=MAGIC,
                                op0=Alu.mult, op1=Alu.add)
                v.tensor_scalar(out=WQTMP[:, pb, 0:width], in0=WQTMP[:, pb, 0:width],
                                scalar1=MAGIC, scalar2=-128.0,
                                op0=Alu.subtract, op1=Alu.max)
                inc(v.tensor_scalar(out=WQI[:, pb, 0:width], in0=WQTMP[:, pb, 0:width],
                                    scalar1=127.0, scalar2=None, op0=Alu.min), "dv")
            qv_done[pb] = cnt["dv"]

            @block.sync
            def _(sync, scr=scr, ch=ch, width=width, pb=pb, dvq=qv_done[pb]):
                wat(sync, "dv", dvq)
                inc(sync.dma_start(out=scr[128 * ch:128 * (ch + 1), :],
                                   in_=WQI[:, pb, 0:width]), "st")
            qst_done[pb] = cnt["st"]

        @block.gpsimd
        def _(g):
            wat(g, "st", cnt["st"])
            inc(g.dma_start(out=SCR[0:1, 16:17], in_=clipx_h), "gd")
        gd_wq = cnt["gd"]

        for ch in range(8):
            @block.tensor
            def _(t, ch=ch):
                wat(t, "dv", cnt["dv"])
                inc(t.transpose(P[5].ap()[0:1, 0:128], in_=WSC[:, 44 + ch:44 + ch + 1],
                                identity=IDENT[:]), "te")
            te_w = cnt["te"]

            @block.vector
            def _(v, ch=ch, te_w=te_w):
                wat(v, "te", te_w)
                inc(v.tensor_copy(out=SWDR[0:1, 128 * ch:128 * (ch + 1)],
                                  in_=P[5].ap()[0:1, 0:128]), "dv")
        dv_swd = cnt["dv"]

        sb_close("WQI", "WQTMP")
        WGTS = sb_open("WGTS", [128, 2, 8, 512], BF16)
        WUTS = sb_open("WUTS", [128, 2, 8, 512], BF16)
        SIG = sb_open("SIG", [128, T])
        GRE = sb_open("GRE", [128, T])
        URE = sb_open("URE", [128, T])
        ALP = sb_open("ALP", [128, 2])

        gu_blocks = [(0, 512), (512, 512), (1024, 512), (1536, 512), (2048, 512),
                     (2560, 256)]
        gb_gd = [0, 0]
        last_g_dv = [0]
        buf_free_ac = [0, 0]
        blk_ld = []
        for bi, (i0, width) in enumerate(gu_blocks):
            pb = bi % 2
            @block.sync
            def _(sync, bi=bi, pb=pb, i0=i0, width=width):
                wat(sync, "gd", gd_wq)
                wat(sync, "dv", dv_swd)
                wat(sync, "ac", buf_free_ac[pb])
                for dch in range(8):
                    inc(sync.dma_start(out=WGTS[:, pb, dch, 0:width],
                                       in_=scr_wg[i0:i0 + width, 128 * dch:128 * (dch + 1)],
                                       transpose=True), "ld")
                    inc(sync.dma_start(out=WUTS[:, pb, dch, 0:width],
                                       in_=scr_wu[i0:i0 + width, 128 * dch:128 * (dch + 1)],
                                       transpose=True), "ld")
            blk_ld.append(cnt["ld"])

            for sub in range(width // 128):
                ic = i0 // 128 + sub
                @block.vector
                def _(v, ic=ic):
                    wat(v, "dv", dv_sxb)
                    wat(v, "ac", cnt["ac"])
                    wat(v, "gd", cnt["gd"])
                    v.tensor_mul(out=ALP[:, 0:1], in0=WSC[:, ic:ic + 1], in1=SCL[:, 1:2])
                    v.tensor_mul(out=ALP[:, 1:2], in0=WSC[:, 22 + ic:22 + ic + 1], in1=SCL[:, 1:2])
                    inc(v.engine_nop(), "dv")
                dv_alp = cnt["dv"]

                @block.tensor
                def _(t, ic=ic, pb=pb, sub=sub, dv_alp=dv_alp, bi=bi):
                    wat(t, "ld", ld_xqf)
                    wat(t, "ld", blk_ld[bi])
                    wat(t, "dv", dv_sxb)
                    wat(t, "ac", cnt["ac"])
                    for j in range(4):
                        for dch in range(8):
                            inc(t.matmul(P[j].ap()[0:128, 0:512],
                                         lhsT=WGTS[:, pb, dch, 128 * sub:128 * (sub + 1)],
                                         rhs=XQF[:, dch, 512 * j:512 * (j + 1)],
                                         start=(dch == 0), stop=(dch == 7)), "te")
                    for j in range(4):
                        for dch in range(8):
                            inc(t.matmul(P[4 + j].ap()[0:128, 0:512],
                                         lhsT=WUTS[:, pb, dch, 128 * sub:128 * (sub + 1)],
                                         rhs=XQF[:, dch, 512 * j:512 * (j + 1)],
                                         start=(dch == 0), stop=(dch == 7)), "te")
                te_gu = cnt["te"]

                @block.scalar
                def _(s, ic=ic, te_gu=te_gu, dv_alp=dv_alp, pg=last_g_dv[0]):
                    wat(s, "te", te_gu)
                    wat(s, "dv", dv_alp)
                    wat(s, "dv", pg)
                    for j in range(4):
                        inc(s.activation(out=SIG[:, 512 * j:512 * (j + 1)],
                                         in_=P[j].ap()[0:128, 0:512], func=AF.Sigmoid,
                                         scale=ALP[:, 0:1]), "ac")
                        inc(s.activation(out=GRE[:, 512 * j:512 * (j + 1)],
                                         in_=P[j].ap()[0:128, 0:512], func=AF.Copy,
                                         scale=ALP[:, 0:1]), "ac")
                        inc(s.activation(out=URE[:, 512 * j:512 * (j + 1)],
                                         in_=P[4 + j].ap()[0:128, 0:512], func=AF.Copy,
                                         scale=ALP[:, 1:2]), "ac")
                ac_gu = cnt["ac"]

                @block.vector
                def _(v, ic=ic, ac_gu=ac_gu):
                    wat(v, "ac", ac_gu)
                    wat(v, "gd", gb_gd[ic % 2])
                    v.tensor_mul(out=GBUF[:, ic % 2, :], in0=GRE[:], in1=SIG[:])
                    v.tensor_mul(out=GBUF[:, ic % 2, :], in0=GBUF[:, ic % 2, :], in1=URE[:])
                    v.reduce_max(out=GMX[:, ic:ic + 1], in_=GBUF[:, ic % 2, :], axis=AX.X,
                                 apply_absolute_value=True)
                    inc(v.engine_nop(), "dv")
                dv_g = cnt["dv"]
                last_g_dv[0] = dv_g

                @block.gpsimd
                def _(g, ic=ic, dv_g=dv_g):
                    wat(g, "dv", dv_g)
                    inc(g.dma_start(out=scr_g[128 * ic:128 * (ic + 1), :],
                                    in_=GBUF[:, ic % 2, :]), "gd")
                gb_gd[ic % 2] = cnt["gd"]
            buf_free_ac[pb] = cnt["ac"]
        gd_g = cnt["gd"]

        @block.vector
        def _(v):
            wt(v, "dv")
            inc(v.reduce_max(out=GMX[:, 30:31], in_=GMX[:, 0:NIC], axis=AX.X), "dv")
        dv_gm = cnt["dv"]

        @block.tensor
        def _(t):
            wat(t, "dv", dv_gm)
            inc(t.transpose(P[5].ap()[0:1, 0:128], in_=GMX[:, 30:31], identity=IDENT[:]), "te")
        te_gm = cnt["te"]

        @block.vector
        def _(v):
            wat(v, "te", te_gm)
            inc(v.tensor_copy(out=CCS[:], in_=P[5].ap()[0:1, 0:128]), "dv")
            vf(v)
            inc(v.reduce_max(out=SCR[0:1, 8:9], in_=CCS[:], axis=AX.X), "dv")
            vf(v)
            inc(v.tensor_scalar(out=CCS[:], in0=ONES1[0:1, 0:128], scalar1=SCR[0:1, 8:9],
                                scalar2=None, op0=Alu.mult), "dv")
        dv_gm2 = cnt["dv"]

        @block.sync
        def _(sync):
            wat(sync, "dv", dv_gm2)
            inc(sync.dma_start(out=cc_m2_in, in_=CCS[:]), "st")
        st_m2 = cnt["st"]

        @block.gpsimd
        def _(g):
            wat(g, "st", st_m2)
            inc(g.collective_compute("AllReduce", Alu.max,
                                     replica_groups=[list(range(8))],
                                     ins=[cc_m2_in.opt()], outs=[cc_m2_out.opt()]), "cc", 1)
            wat(g, "cc", cnt["cc"])
            inc(g.dma_start(out=CCS[:], in_=cc_m2_out), "gd")
        gd_m2 = cnt["gd"]

        @block.scalar
        def _(s):
            wat(s, "ld", ld_in)
            inc(s.activation(out=SCR[0:1, 9:10], in_=SCR[0:1, 1:2], func=AF.Sigmoid), "ac")
        ac_sig2 = cnt["ac"]

        @block.vector
        def _(v):
            wat(v, "gd", gd_m2)
            wat(v, "ac", ac_sig2)
            v.scalar_tensor_tensor(out=SCR[0:1, 10:11], in0=SCR[0:1, 9:10],
                                   scalar=1.0 / QMAX, in1=CCS[0:1, 0:1],
                                   op0=Alu.mult, op1=Alu.mult)
            vf(v)
            v.reciprocal(out=SCR[0:1, 11:12], in_=SCR[0:1, 10:11])
            inc(v.engine_nop(), "dv")
        dv_sd = cnt["dv"]

        @block.tensor
        def _(t):
            wat(t, "dv", dv_sd)
            inc(t.matmul(P[5].ap()[0:128, 0:1], lhsT=ONES1[0:1, 0:128], rhs=SCR[0:1, 11:12],
                         start=True, stop=True), "te")
            inc(t.matmul(P[5].ap()[0:128, 8:9], lhsT=ONES1[0:1, 0:128], rhs=SCR[0:1, 10:11],
                         start=True, stop=True), "te")
        te_sd = cnt["te"]

        @block.vector
        def _(v):
            wat(v, "te", te_sd)
            v.tensor_copy(out=SCL[:, 2:3], in_=P[5].ap()[0:128, 0:1])
            inc(v.tensor_copy(out=SCL[:, 3:4], in_=P[5].ap()[0:128, 8:9]), "dv")
        dv_sdb = cnt["dv"]

        te_gu_all = cnt["te"]
        sb_close("ALP", "URE", "GRE", "SIG", "WUTS", "WGTS", "XQF")
        HQF = sb_open("HQF", [128, NIC, T], BF16)
        WDT = sb_open("WDT", [128, NIC, D], BF16)
        EVO = sb_open("EVO", [128, 2, D])

        @block.sync
        def _(sync):
            wat(sync, "gd", gd_wq)
            wat(sync, "te", te_gu_all)
            wat(sync, "dv", dv_sdb)
            for ic in range(NIC):
                inc(sync.dma_start(out=WDT[:, ic, :], in_=scr_wd[:, 128 * ic:128 * (ic + 1)],
                                   transpose=True), "ld")
        ld_wdt = cnt["ld"]

        for jj in range(2):
            @block.tensor
            def _(t, jj=jj):
                wat(t, "dv", dv_swd)
                wat(t, "dv", cnt["dv"])
                inc(t.matmul(P[7].ap()[0:128, 0:512], lhsT=ONES1[0:1, 0:128],
                             rhs=SWDR[0:1, 512 * jj:512 * (jj + 1)],
                             start=True, stop=True), "te")
            te_j = cnt["te"]

            @block.vector
            def _(v, jj=jj, te_j=te_j):
                wat(v, "te", te_j)
                wat(v, "dv", dv_sdb)
                v.tensor_tensor(out=GBSC[:, 512 * jj:512 * (jj + 1)],
                                in0=GBB[:, 512 * jj:512 * (jj + 1)],
                                in1=P[7].ap()[0:128, 0:512],
                                op=Alu.mult)
                inc(v.tensor_scalar(out=GBSC[:, 512 * jj:512 * (jj + 1)],
                                    in0=GBSC[:, 512 * jj:512 * (jj + 1)],
                                    scalar1=SCL[:, 3:4], scalar2=None, op0=Alu.mult), "dv")
        dv_gbsc = cnt["dv"]

        for ic in range(NIC):
            @block.gpsimd
            def _(g, ic=ic):
                wat(g, "gd", gd_g)
                wat(g, "dv", cnt["dv"])
                inc(g.dma_start(out=GBUF[:, ic % 2, :],
                                in_=scr_g[128 * ic:128 * (ic + 1), :]), "gd")
            gd_r = cnt["gd"]

            @block.vector
            def _(v, ic=ic, gd_r=gd_r):
                wat(v, "gd", gd_r)
                wat(v, "dv", dv_sdb)
                v.tensor_scalar(out=GBUF[:, ic % 2, :], in0=GBUF[:, ic % 2, :],
                                scalar1=SCL[:, 2:3],
                                scalar2=MAGIC, op0=Alu.mult, op1=Alu.add)
                v.tensor_scalar(out=GBUF[:, ic % 2, :], in0=GBUF[:, ic % 2, :],
                                scalar1=MAGIC,
                                scalar2=-128.0, op0=Alu.subtract, op1=Alu.max)
                inc(v.tensor_scalar(out=HQF[:, ic, :], in0=GBUF[:, ic % 2, :],
                                    scalar1=127.0,
                                    scalar2=None, op0=Alu.min), "dv")
        dv_hq = cnt["dv"]

        prev_ev = dv_gbsc
        st_tiles = [0] * NT
        for tc in range(NT):
            @block.gpsimd
            def _(g, tc=tc):
                wat(g, "st", st_x2)
                wat(g, "dv", cnt["dv"])
                inc(g.dma_start(out=XRB[:, tc % 2, :],
                                in_=scr_x2[128 * tc:128 * (tc + 1), :]), "gd")
            gd_x2 = cnt["gd"]

            @block.tensor
            def _(t, tc=tc):
                wat(t, "dv", dv_hq)
                wat(t, "ld", ld_wdt)
                wat(t, "dv", prev_ev)
                for dh in range(2):
                    for ic in range(NIC):
                        inc(t.matmul(P[2 * (tc % 2) + dh].ap()[0:128, 0:512],
                                     lhsT=HQF[:, ic, 128 * tc:128 * (tc + 1)],
                                     rhs=WDT[:, ic, 512 * dh:512 * (dh + 1)],
                                     start=(ic == 0), stop=(ic == NIC - 1)), "te")
            te_dn = cnt["te"]

            @block.vector
            def _(v, tc=tc, te_dn=te_dn, gd_x2=gd_x2):
                wat(v, "te", te_dn)
                wat(v, "gd", gd_x2)
                wat(v, "dv", dv_gbsc)
                if tc >= 2:
                    wat(v, "st", st_tiles[tc - 2])
                for dh in range(2):
                    v.tensor_tensor(out=EVO[:, tc % 2, 512 * dh:512 * (dh + 1)],
                                    in0=P[2 * (tc % 2) + dh].ap()[0:128, 0:512],
                                    in1=GBSC[:, 512 * dh:512 * (dh + 1)], op=Alu.mult)
                inc(v.tensor_add(out=EVO[:, tc % 2, :], in0=EVO[:, tc % 2, :],
                                 in1=XRB[:, tc % 2, :]), "dv")
            prev_ev = cnt["dv"]

            @block.sync
            def _(sync, tc=tc, prev_ev=prev_ev):
                wat(sync, "dv", prev_ev)
                inc(sync.dma_start(out=out_h[128 * tc:128 * (tc + 1), :],
                                   in_=EVO[:, tc % 2, :]), "st")
            st_tiles[tc] = cnt["st"]

        @block.sync
        def _(sync):
            wt(sync, "st", "dv", "ac", "te", "gd", "ld")

        for n in reversed(list(live.keys())):
            sb_close(n)

    return nc


def kernel(L0_attn_hidden_states, L0_attn_Wq, L0_attn_Wk, L0_attn_Wv, L0_attn_Wo,
           L0_ffn_clip_x, L0_ffn_clip_down, L0_ffn_w_gate, L0_ffn_w_up,
           L0_ffn_w_down, L0_c, L0_wmod):
    x = np.ascontiguousarray(L0_attn_hidden_states, dtype=np.float32)
    nc = build_nc(DBG, STAGE, CUT)
    base = {
        "wq": np.ascontiguousarray(L0_attn_Wq, np.float32),
        "wk": np.ascontiguousarray(L0_attn_Wk, np.float32),
        "wv": np.ascontiguousarray(L0_attn_Wv, np.float32),
        "wo": np.ascontiguousarray(L0_attn_Wo, np.float32),
        "wg": np.ascontiguousarray(L0_ffn_w_gate, np.float32),
        "wu": np.ascontiguousarray(L0_ffn_w_up, np.float32),
        "wd": np.ascontiguousarray(L0_ffn_w_down, np.float32),
        "cvec": np.ascontiguousarray(L0_c, np.float32),
        "clipx": np.asarray(L0_ffn_clip_x, np.float32).reshape(1, 1),
        "clipd": np.asarray(L0_ffn_clip_down, np.float32).reshape(1, 1),
    }
    wmod = np.ascontiguousarray(L0_wmod, np.float32)
    in_maps = []
    for b in range(8):
        m = dict(base)
        m["x"] = np.ascontiguousarray(x[b])
        m["wmod"] = np.ascontiguousarray(wmod[128 * b:128 * (b + 1)])
        in_maps.append(m)
    res = run_bass_kernel_spmd(nc, in_maps, core_ids=list(range(8)))
    out = np.stack([res.results[b]["out"] for b in range(8)], axis=0)
    globals()["_last_results"] = res
    return out


# version 11
# speedup vs baseline: 1.0221x; 1.0221x over previous
import numpy as np
from contextlib import ExitStack

import concourse.bass as bass
import concourse.mybir as mybir
from concourse.bass_utils import run_bass_kernel_spmd

F32 = mybir.dt.float32
BF16 = mybir.dt.bfloat16
U32 = mybir.dt.uint32
Alu = mybir.AluOpType
AF = mybir.ActivationFunctionType
AX = mybir.AxisListType

T = 2048
D = 1024
NH = 16
NI = 2816
NT = 16
NC2 = 8
NIC = NI // 128
ALPHA = float(32.0 ** -0.25)
BETA = float(2.0 ** 0.5)
SQ_BIAS = float(1.0 / 2.0 ** 0.5)
MAGIC = 12582912.0
QMAX = 127.0
EPS_LN = 1e-5
EPS_ATT = 1e-12

STAGE = 3
DBG = ()
CUT = None


def build_nc(dbg=(), stage=STAGE, cut=None):
    nc = bass.Bass("TRN2", target_bir_lowering=False)

    x_h = nc.dram_tensor("x", [T, D], F32, kind="ExternalInput").ap()
    wq_h = nc.dram_tensor("wq", [D, 256], F32, kind="ExternalInput").ap()
    wk_h = nc.dram_tensor("wk", [D, 256], F32, kind="ExternalInput").ap()
    wv_h = nc.dram_tensor("wv", [D, D], F32, kind="ExternalInput").ap()
    wo_h = nc.dram_tensor("wo", [D, D], F32, kind="ExternalInput").ap()
    wg_h = nc.dram_tensor("wg", [NI, D], F32, kind="ExternalInput").ap()
    wu_h = nc.dram_tensor("wu", [NI, D], F32, kind="ExternalInput").ap()
    wd_h = nc.dram_tensor("wd", [D, NI], F32, kind="ExternalInput").ap()
    cvec_h = nc.dram_tensor("cvec", [8, D], F32, kind="ExternalInput").ap()
    wmod_h = nc.dram_tensor("wmod", [128, 6 * D], F32, kind="ExternalInput").ap()
    clipx_h = nc.dram_tensor("clipx", [1, 1], F32, kind="ExternalInput").ap()
    clipd_h = nc.dram_tensor("clipd", [1, 1], F32, kind="ExternalInput").ap()
    out_h = nc.dram_tensor("out", [T, D], F32, kind="ExternalOutput").ap()

    dbg_h = {}
    dbg_shapes = {"q": ([256, T], BF16), "k": ([256, T], BF16),
                  "v": ([T, NH * 64], BF16),
                  "x2": ([T, D], F32), "mod": ([1, 6 * 1024], F32),
                  "xh": ([T, D], BF16), "ln": ([128, 96], F32),
                  "xt": ([T, D], F32), "zz": ([128, 512], F32), "dd": ([128, 64], F32),
                  "yr": ([64, 8 * 1024], BF16), "qh": ([16, 4 * T], BF16),
                  "am": ([128, 512], BF16), "uq": ([128, 512], BF16),
                  "ys": ([128, 512], BF16),
                  "stf": ([128, 520], BF16), "st1": ([1, 260], BF16),
                  "st2": ([16, 260], BF16), "qfa": ([128, 256], BF16),
                  "qfb": ([128, 256], BF16)}
    for name in dbg:
        shp, dt = dbg_shapes[name]
        dbg_h[name] = nc.dram_tensor("dbg_" + name, shp, dt,
                                     kind="ExternalOutput").ap()

    cc_mod_in = nc.dram_tensor("cc_mod_in", [8, 6 * D], F32, kind="Internal").ap()
    cc_mod_out = nc.dram_tensor("cc_mod_out", [8, 6 * D], F32, kind="Internal").ap()
    cc_m1_in = nc.dram_tensor("cc_m1_in", [1, 128], F32, kind="Internal").ap()
    cc_m1_out = nc.dram_tensor("cc_m1_out", [1, 128], F32, kind="Internal").ap()
    cc_m2_in = nc.dram_tensor("cc_m2_in", [1, 128], F32, kind="Internal").ap()
    cc_m2_out = nc.dram_tensor("cc_m2_out", [1, 128], F32, kind="Internal").ap()
    scr_xhat = nc.dram_tensor("scr_xhat", [T, D], BF16, kind="Internal").ap()
    scr_xq = nc.dram_tensor("scr_xq", [T, D], BF16, kind="Internal").ap()
    scr_wg = nc.dram_tensor("scr_wg", [NI, D], BF16, kind="Internal").ap()
    scr_wu = nc.dram_tensor("scr_wu", [NI, D], BF16, kind="Internal").ap()
    scr_wd = nc.dram_tensor("scr_wd", [D, NI], BF16, kind="Internal").ap()
    scr_g = nc.dram_tensor("scr_g", [NI, T], F32, kind="Internal").ap()
    scr_x2 = nc.dram_tensor("scr_x2", [T, D], F32, kind="Internal").ap()

    es = ExitStack()
    with es:
        def sb(name, shape, dtype=F32):
            return es.enter_context(nc.sbuf_tensor(name, shape, dtype))

        live = {}

        def sb_open(name, shape, dtype=F32):
            cm = nc.sbuf_tensor(name, shape, dtype)
            t = cm.__enter__()
            live[name] = cm
            return t

        def sb_close(*names):
            for n in names:
                live.pop(n).__exit__(None, None, None)

        PIDT = sb("PIDT", [1, 1], U32)
        ONES1 = sb("ONES1", [1, 128])
        ONESB = sb("ONESB", [128, 1], BF16)
        ONESBR = sb("ONESBR", [1, 512], BF16)
        ZROWB = sb("ZROWB", [1, 128], BF16)
        IDENT = sb("IDENT", [128, 128])
        MSTT = sb("MSTT", [128, 512], BF16)
        ESTRA = sb("ESTRA", [16, 128], BF16)
        ESTRB = sb("ESTRB", [16, 128], BF16)
        ETILE = sb("ETILE", [16, 128], BF16)
        MODF = sb("MODF", [128, 8, 2])
        ROWTMP = sb("ROWTMP", [1, 512])
        CB1B = sb("CB1B", [128, D])
        SBB = sb("SBB", [128, D])
        GAB = sb("GAB", [128, D])
        GBB = sb("GBB", [128, D])
        SCL = sb("SCL", [128, 8])
        SCR = sb("SCR", [1, 32])
        STATM = sb("STATM", [128, NT])
        STATR = sb("STATR", [128, NT])
        STTMP = sb("STTMP", [128, NT])
        BNS = sb("BNS", [128, 2 * NT])
        GMX = sb("GMX", [128, 32])
        CCS = sb("CCS", [1, 128])
        WSC = sb("WSC", [128, 64])
        EVS = sb("EVS", [128, 64])
        SQB = sb("SQB", [128, 1])

        XT = sb_open("XT", [128, NT, D])

        XHB = sb_open("XHB", [128, NT, D], BF16)
        WMODS = sb_open("WMODS", [128, 6 * D])
        MODP = sb_open("MODP", [8, 6 * D])
        CV = sb_open("CV", [8, D])
        CVS = sb_open("CVS", [8, D])
        CVT = sb_open("CVT", [128, 8])
        MODR = sb_open("MODR", [1, 6 * D])
        TIN = sb_open("TIN", [8, 128])

        P = [es.enter_context(nc.psum_tensor(f"P{i}", [128, 512], F32, bank_id=i))
             for i in range(8)]

        semh, cnt = {}, {}
        for s in ["ld", "gd", "te", "dv", "ac", "st", "cc"]:
            semh[s] = es.enter_context(nc.semaphore(s))
            cnt[s] = 0

        def inc(instr, s, n=None):
            if n is None:
                n = 16 if s in ("ld", "gd", "st") else 1
            cnt[s] += n
            instr.then_inc(semh[s], n)
            return instr

        def wt(eng, *sems):
            for s in sems:
                if cnt[s] > 0:
                    eng.wait_ge(semh[s], cnt[s])

        def wat(eng, s, v):
            if v > 0:
                eng.wait_ge(semh[s], v)

        def vf(v):
            inc(v.engine_nop(), "dv")
            wat(v, "dv", cnt["dv"])

        block = es.enter_context(nc.Block())

        def finish():
            @block.sync
            def _(sync):
                wt(sync, "st", "dv", "ac", "te", "gd", "ld", "cc")
            for n in reversed(list(live.keys())):
                sb_close(n)

        @block.sync
        def _(sync):
            inc(sync.dma_start(out=XT[:], in_=x_h.rearrange("(c p) d -> p c d", p=128)), "ld")
            inc(sync.dma_start(out=PIDT[:], in_=nc.partition_id_tensor.ap()), "ld")
            inc(sync.dma_start(out=SCR[0:1, 0:1], in_=clipx_h), "ld")
            inc(sync.dma_start(out=SCR[0:1, 1:2], in_=clipd_h), "ld")
            inc(sync.dma_start(out=CV[:], in_=cvec_h), "ld")
        ld_in = cnt["ld"]

        @block.gpsimd
        def _(g):
            inc(g.dma_start(out=WMODS[:], in_=wmod_h), "gd")
        gd_wmod = cnt["gd"]

        @block.vector
        def _(v):
            v.memset(ONES1[:], 1.0)
            v.memset(ONESB[:], 1.0)
            v.memset(ONESBR[:], 1.0)
            v.memset(ZROWB[:], 0.0)
            v.memset(IDENT[:], 1.0)
            v.memset(MSTT[:], 1.0)
            v.memset(ESTRA[:], 1.0)
            v.memset(ESTRB[:], 1.0)
            v.memset(ETILE[:], 1.0)
            v.memset(SQB[:], SQ_BIAS)
            inc(v.engine_nop(), "dv")
        dv_const = cnt["dv"]

        @block.gpsimd
        def _(g):
            wat(g, "dv", dv_const)
            g.affine_select(out=IDENT[:], in_=IDENT[:], pattern=[[-1, 128]],
                            compare_op=Alu.is_ge, fill=0.0, base=0, channel_multiplier=1)
            g.affine_select(out=IDENT[:], in_=IDENT[:], pattern=[[1, 128]],
                            compare_op=Alu.is_ge, fill=0.0, base=0, channel_multiplier=-1)
            g.affine_select(out=MSTT[:], in_=MSTT[:], pattern=[[-128, 2], [1, 256]],
                            compare_op=Alu.is_ge, fill=0.0, base=0, channel_multiplier=-1)
            g.affine_select(out=ESTRA[:], in_=ESTRA[:], pattern=[[-1, 8], [0, 16]],
                            compare_op=Alu.is_ge, fill=0.0, base=0, channel_multiplier=1)
            g.affine_select(out=ESTRA[:], in_=ESTRA[:], pattern=[[1, 8], [0, 16]],
                            compare_op=Alu.is_ge, fill=0.0, base=0, channel_multiplier=-1)
            g.affine_select(out=ESTRB[:], in_=ESTRB[:], pattern=[[-1, 8], [0, 16]],
                            compare_op=Alu.is_ge, fill=0.0, base=-8, channel_multiplier=1)
            g.affine_select(out=ESTRB[:], in_=ESTRB[:], pattern=[[1, 8], [0, 16]],
                            compare_op=Alu.is_ge, fill=0.0, base=8, channel_multiplier=-1)
            g.affine_select(out=ETILE[:], in_=ETILE[:], pattern=[[0, 8], [-1, 16]],
                            compare_op=Alu.is_ge, fill=0.0, base=0, channel_multiplier=1)
            g.affine_select(out=ETILE[:], in_=ETILE[:], pattern=[[0, 8], [1, 16]],
                            compare_op=Alu.is_ge, fill=0.0, base=0, channel_multiplier=-1)
            inc(g.engine_nop(), "gd", 1)
        gd_masks = cnt["gd"]
        if cut == "const":
            finish()
            return nc

        @block.scalar
        def _(s):
            wat(s, "ld", ld_in)
            inc(s.activation(out=CVS[:], in_=CV[:], func=AF.Silu), "ac")
        ac_cvs = cnt["ac"]

        @block.gpsimd
        def _(g):
            wat(g, "ac", ac_cvs)
            with g.register("pid") as pid:
                g.reg_load(pid, PIDT[0:1, 0:1])
                for k in range(8):
                    with g.If_eq(pid, k):
                        g.dma_start(out=TIN[0:8, 0:128],
                                    in_=CVS[:, 128 * k:128 * (k + 1)]).then_inc(semh["gd"], 16)
            cnt["gd"] += 16
        gd_cvt = cnt["gd"]

        @block.tensor
        def _(t):
            wat(t, "gd", gd_masks)
            wat(t, "gd", gd_cvt)
            inc(t.transpose(P[0].ap()[0:128, 0:8], in_=TIN[0:8, 0:128],
                            identity=IDENT[0:8, 0:8]), "te")
        te_cvt = cnt["te"]

        @block.vector
        def _(v):
            wat(v, "te", te_cvt)
            inc(v.tensor_copy(out=CVT[:], in_=P[0].ap()[0:128, 0:8]), "dv")
        dv_cvt = cnt["dv"]

        mm_done = []
        @block.tensor
        def _(t):
            wat(t, "dv", dv_cvt)
            wat(t, "gd", gd_wmod)
            for j in range(12):
                bank = P[1 + (j % 4)]
                if j >= 4:
                    wat(t, "dv", dv_cvt + (j - 4) + 1)
                inc(t.matmul(bank.ap()[0:8, 0:512], lhsT=CVT[:],
                             rhs=WMODS[:, 512 * j:512 * (j + 1)], start=True, stop=True), "te")
                mm_done.append(cnt["te"])

        @block.vector
        def _(v):
            for j in range(12):
                wat(v, "te", mm_done[j])
                inc(v.tensor_copy(out=MODP[:, 512 * j:512 * (j + 1)],
                                  in_=P[1 + (j % 4)].ap()[0:8, 0:512]), "dv")
        dv_modp = cnt["dv"]

        @block.sync
        def _(sync):
            wat(sync, "dv", dv_modp)
            inc(sync.dma_start(out=cc_mod_in, in_=MODP[:]), "st")
        st_ccm = cnt["st"]

        @block.gpsimd
        def _(g):
            wat(g, "st", st_ccm)
            inc(g.collective_compute("AllReduce", Alu.add,
                                     replica_groups=[list(range(8))],
                                     ins=[cc_mod_in.opt()], outs=[cc_mod_out.opt()]), "cc", 1)
            wat(g, "cc", cnt["cc"])
            with g.register("pid2") as pid:
                g.reg_load(pid, PIDT[0:1, 0:1])
                for k in range(8):
                    with g.If_eq(pid, k):
                        g.dma_start(out=MODR[0:1, 0:6 * D],
                                    in_=cc_mod_out[k:k + 1, :]).then_inc(semh["gd"], 16)
            cnt["gd"] += 16
        gd_modr = cnt["gd"]

        @block.vector
        def _(v):
            wat(v, "gd", gd_modr)
            sa = MODR[0:1, 0:D]
            ca = MODR[0:1, D:2 * D]
            ga = MODR[0:1, 2 * D:3 * D]
            sbr = MODR[0:1, 3 * D:4 * D]
            cb = MODR[0:1, 4 * D:5 * D]
            gb = MODR[0:1, 5 * D:6 * D]
            t1 = MODP[0:1, 0:D]
            t2 = MODP[0:1, D:2 * D]
            v.tensor_scalar(out=ca, in0=ca, scalar1=1.0, scalar2=None, op0=Alu.add)
            v.tensor_scalar(out=cb, in0=cb, scalar1=1.0, scalar2=None, op0=Alu.add)
            for r in (ga, gb):
                v.tensor_scalar(out=t1, in0=r, scalar1=0.0, scalar2=2.0,
                                op0=Alu.is_ge, op1=Alu.mult)
                v.tensor_scalar(out=t1, in0=t1, scalar1=1.0, scalar2=None, op0=Alu.subtract)
                v.tensor_scalar(out=t2, in0=r, scalar1=-1.0, scalar2=None, op0=Alu.mult)
                v.tensor_tensor(out=t2, in0=t2, in1=r, op=Alu.max)
                v.tensor_scalar(out=t2, in0=t2, scalar1=0.25, scalar2=None, op0=Alu.max)
                v.tensor_mul(out=r, in0=t1, in1=t2)
            v.reciprocal(out=t1, in_=ca)
            v.tensor_mul(out=sa, in0=sa, in1=t1)
            inc(v.engine_nop(), "dv")
        dv_modv = cnt["dv"]

        if "mod" in dbg_h:
            @block.sync
            def _(sync):
                wat(sync, "dv", dv_modv)
                inc(sync.dma_start(out=dbg_h["mod"], in_=MODR[0:1, 0:6 * D]), "st")

        if cut == "mod":
            finish()
            return nc

        prev = dv_modv
        for j, coloff in ((0, 0), (1, D)):
            for c in range(8):
                @block.tensor
                def _(t, j=j, coloff=coloff, c=c):
                    wat(t, "dv", prev)
                    inc(t.transpose(P[5].ap()[0:128, 0:1],
                                    in_=MODR[0:1, coloff + 128 * c:coloff + 128 * (c + 1)],
                                    identity=IDENT[0:1, 0:1]), "te")
                te_x = cnt["te"]

                @block.vector
                def _(v, j=j, c=c, te_x=te_x):
                    wat(v, "te", te_x)
                    inc(v.tensor_copy(out=MODF[:, c, j:j + 1], in_=P[5].ap()[0:128, 0:1]), "dv")
                prev = cnt["dv"]
        dv_modf = cnt["dv"]

        prev_bc = dv_modv
        for dstt, coloff in ((CB1B, 4 * D), (SBB, 3 * D), (GAB, 2 * D), (GBB, 5 * D)):
            for jj in range(2):
                @block.sync
                def _(sync, coloff=coloff, jj=jj):
                    wat(sync, "dv", prev_bc)
                    wat(sync, "te", cnt["te"])
                    inc(sync.dma_start(out=ROWTMP[:],
                                       in_=MODR[0:1, coloff + 512 * jj:coloff + 512 * (jj + 1)]), "st")
                st_r = cnt["st"]

                @block.tensor
                def _(t, jj=jj, st_r=st_r):
                    wat(t, "st", st_r)
                    wat(t, "dv", prev_bc)
                    inc(t.matmul(P[6].ap()[0:128, 0:512], lhsT=ONES1[0:1, 0:128],
                                 rhs=ROWTMP[0:1, 0:512],
                                 start=True, stop=True), "te")
                te_x = cnt["te"]

                @block.vector
                def _(v, dstt=dstt, jj=jj, te_x=te_x):
                    wat(v, "te", te_x)
                    inc(v.tensor_copy(out=dstt[:, 512 * jj:512 * (jj + 1)],
                                      in_=P[6].ap()[0:128, 0:512]), "dv")
                prev_bc = cnt["dv"]
        dv_bcasts = cnt["dv"]
        if cut == "modf":
            finish()
            return nc

        @block.vector
        def _(v):
            wat(v, "ld", ld_in)
            for c in range(NT):
                v.tensor_scalar(out=XHB[:, c, :], in0=XT[:, c, :], scalar1=1.0,
                                scalar2=0.0, op0=Alu.mult, op1=Alu.add,
                                accum_out=STATM[:, c:c + 1])
                v.scalar_tensor_tensor(out=XHB[:, c, :], in0=XT[:, c, :], scalar=1.0,
                                       in1=XT[:, c, :], op0=Alu.mult, op1=Alu.mult,
                                       accum_out=STTMP[:, c:c + 1])
            vf(v)
            v.tensor_scalar(out=STATM[:], in0=STATM[:], scalar1=1.0 / D, scalar2=None,
                            op0=Alu.mult)
            v.tensor_scalar(out=STTMP[:], in0=STTMP[:], scalar1=1.0 / D, scalar2=EPS_LN,
                            op0=Alu.mult, op1=Alu.add)
            vf(v)
            v.tensor_mul(out=BNS[:, 0:NT], in0=STATM[:], in1=STATM[:])
            vf(v)
            v.tensor_tensor(out=STTMP[:], in0=STTMP[:], in1=BNS[:, 0:NT],
                            op=Alu.subtract)
            vf(v)
            v.reciprocal(out=STATR[:], in_=STTMP[:])
            if "ln" in dbg_h:
                vf(v)
                v.tensor_copy(out=BNS[:, 16:32], in_=STATR[:])
            inc(v.engine_nop(), "dv")
        dv_bn1 = cnt["dv"]

        @block.scalar
        def _(s):
            wat(s, "dv", dv_bn1)
            inc(s.activation(out=STATR[:], in_=STATR[:], func=AF.Sqrt), "ac")
        ac_sq1 = cnt["ac"]

        st_lndbg = 0
        if "ln" in dbg_h:
            @block.sync
            def _(sync):
                wat(sync, "ac", ac_sq1)
                inc(sync.dma_start(out=dbg_h["ln"][:, 48:64], in_=BNS[:, 16:32]), "st")
                inc(sync.dma_start(out=dbg_h["ln"][:, 64:80], in_=STATR[:]), "st")
                wt(sync, "st")
            st_lndbg = cnt["st"]

        @block.vector
        def _(v):
            wat(v, "ac", ac_sq1)
            wat(v, "st", st_lndbg)
            v.tensor_mul(out=BNS[:, 0:NT], in0=STATR[:], in1=STATR[:])
            vf(v)
            v.tensor_mul(out=BNS[:, 0:NT], in0=BNS[:, 0:NT], in1=STTMP[:])
            vf(v)
            v.tensor_scalar(out=BNS[:, 0:NT], in0=BNS[:, 0:NT], scalar1=-0.5,
                            scalar2=1.5, op0=Alu.mult, op1=Alu.add)
            vf(v)
            v.tensor_mul(out=STATR[:], in0=STATR[:], in1=BNS[:, 0:NT])
            vf(v)
            for c in range(NT):
                v.tensor_scalar(out=XHB[:, c, :], in0=XT[:, c, :],
                                scalar1=STATM[:, c:c + 1], scalar2=STATR[:, c:c + 1],
                                op0=Alu.subtract, op1=Alu.mult)
            inc(v.engine_nop(), "dv")
        dv_xhat = cnt["dv"]

        @block.sync
        def _(sync):
            wat(sync, "dv", dv_xhat)
            inc(sync.dma_start(out=scr_xhat.rearrange("(c p) d -> p c d", p=128), in_=XHB[:]), "st")
            wt(sync, "st")
        st_xhat = cnt["st"]
        if "xt" in dbg_h:
            @block.sync
            def _(sync):
                wat(sync, "ld", ld_in)
                inc(sync.dma_start(out=dbg_h["xt"].rearrange("(c p) d -> p c d", p=128),
                                   in_=XT[:]), "st")
        if "ln" in dbg_h:
            @block.sync
            def _(sync):
                wat(sync, "dv", dv_xhat)
                inc(sync.dma_start(out=dbg_h["ln"][:, 0:16], in_=STATM[:]), "st")
                inc(sync.dma_start(out=dbg_h["ln"][:, 16:32], in_=STATR[:]), "st")
                inc(sync.dma_start(out=dbg_h["ln"][:, 32:48], in_=STTMP[:]), "st")
        if "xh" in dbg_h:
            @block.gpsimd
            def _(g):
                wat(g, "st", st_xhat)
                inc(g.dma_start(out=dbg_h["xh"], in_=scr_xhat), "gd")
        if cut == "ln":
            finish()
            return nc

        sb_close("TIN", "MODR", "CVT", "CVS", "CV", "MODP", "WMODS", "XHB")

        QF = sb_open("QF", [128, 2, T], BF16)
        KF = sb_open("KF", [128, 2, T], BF16)
        KTM = sb_open("KTM", [128, NT, 256], BF16)
        VTM = sb_open("VTM", [128, NT, NH * 64], BF16)
        XHT = sb_open("XHT", [128, 8, T], BF16)
        WPA = sb_open("WPA", [128, 8, 1536], BF16)
        WTMP = sb_open("WTMP", [128, 512])
        WSA = sb_open("WSA", [1, 1536], BF16)
        SACOL = sb_open("SACOL", [128, 8], BF16)

        @block.sync
        def _(sync):
            for dch in range(8):
                inc(sync.dma_start(out=XHT[:, dch, :],
                                   in_=scr_xhat[:, 128 * dch:128 * (dch + 1)], transpose=True), "ld")
        ld_xht = cnt["ld"]

        prev_fold = dv_xhat
        for dch in range(8):
            @block.gpsimd
            def _(g, dch=dch):
                wat(g, "dv", prev_fold)
                wat(g, "st", st_xhat)
                inc(g.dma_start(out=WTMP[:, 0:256], in_=wq_h[128 * dch:128 * (dch + 1), :]), "gd")
                inc(g.dma_start(out=WTMP[:, 256:512], in_=wk_h[128 * dch:128 * (dch + 1), :]), "gd")
            gd_w = cnt["gd"]

            @block.vector
            def _(v, dch=dch, gd_w=gd_w):
                wat(v, "gd", gd_w)
                wat(v, "dv", dv_modf)
                v.tensor_scalar(out=WPA[:, dch, 0:512], in0=WTMP[:, 0:512],
                                scalar1=MODF[:, dch, 1:2], scalar2=ALPHA,
                                op0=Alu.mult, op1=Alu.mult)
                inc(v.engine_nop(), "dv")
            dv_qk = cnt["dv"]

            prev_v = dv_qk
            for vh in range(2):
                @block.gpsimd
                def _(g, dch=dch, vh=vh, prev_v=prev_v):
                    wat(g, "dv", prev_v)
                    inc(g.dma_start(out=WTMP[:, 0:512],
                                    in_=wv_h[128 * dch:128 * (dch + 1),
                                             512 * vh:512 * (vh + 1)]), "gd")
                gd_v = cnt["gd"]

                @block.vector
                def _(v, dch=dch, vh=vh, gd_v=gd_v):
                    wat(v, "gd", gd_v)
                    v.tensor_scalar(out=WPA[:, dch, 512 * (vh + 1):512 * (vh + 2)],
                                    in0=WTMP[:, 0:512],
                                    scalar1=MODF[:, dch, 1:2], scalar2=None, op0=Alu.mult)
                    inc(v.engine_nop(), "dv")
                prev_v = cnt["dv"]
            prev_fold = cnt["dv"]

        @block.vector
        def _(v):
            wat(v, "dv", prev_fold)
            inc(v.tensor_copy(out=SACOL[:], in_=MODF[:, :, 0]), "dv")
        dv_sac = cnt["dv"]

        te_wsa = []
        @block.tensor
        def _(t):
            wat(t, "dv", dv_sac)
            for j in range(3):
                for dch in range(8):
                    inc(t.matmul(P[1 + j].ap()[0:1, 0:512], lhsT=SACOL[:, dch:dch + 1],
                                 rhs=WPA[:, dch, 512 * j:512 * (j + 1)],
                                 start=(dch == 0), stop=(dch == 7)), "te")
                te_wsa.append(cnt["te"])

        @block.vector
        def _(v):
            for j in range(3):
                wat(v, "te", te_wsa[j])
                inc(v.tensor_copy(out=WSA[0:1, 512 * j:512 * (j + 1)],
                                  in_=P[1 + j].ap()[0:1, 0:512]), "dv")
        dv_wsa = cnt["dv"]
        if cut == "wprep":
            finish()
            return nc

        bank_evac = [0] * 8

        def fm_proj(dst, col0):
            for fch in range(2):
                for j in range(4):
                    b = fch * 4 + j
                    @block.tensor
                    def _(t, fch=fch, j=j, b=b):
                        wat(t, "dv", dv_wsa)
                        wat(t, "ld", ld_xht)
                        wat(t, "ac", bank_evac[b])
                        bank = P[b]
                        for dch in range(8):
                            inc(t.matmul(bank.ap()[0:128, 0:512],
                                         lhsT=WPA[:, dch, col0 + 128 * fch:col0 + 128 * (fch + 1)],
                                         rhs=XHT[:, dch, 512 * j:512 * (j + 1)],
                                         start=(dch == 0), stop=False), "te")
                        inc(t.matmul(bank.ap()[0:128, 0:512],
                                     lhsT=WSA[0:1, col0 + 128 * fch:col0 + 128 * (fch + 1)],
                                     rhs=ONESBR[0:1, 0:512], start=False, stop=True), "te")
                    fill_te = cnt["te"]

                    @block.scalar
                    def _(s, fch=fch, j=j, b=b, fill_te=fill_te):
                        wat(s, "te", fill_te)
                        inc(s.activation(out=dst[:, fch, 512 * j:512 * (j + 1)],
                                         in_=P[b].ap()[0:128, 0:512], func=AF.Copy), "ac")
                    bank_evac[b] = cnt["ac"]
            return cnt["ac"]

        ac_q = fm_proj(QF, 0)
        ac_k = fm_proj(KF, 256)

        ac_tmv = []
        for c in range(NT):
            @block.tensor
            def _(t, c=c):
                if c == 0:
                    wat(t, "ac", bank_evac[2])
                elif c == 1:
                    wat(t, "ac", bank_evac[5])
                if c >= 2:
                    wat(t, "ac", ac_tmv[c - 2])
                for dch in range(8):
                    inc(t.matmul(P[(c % 2) * 3 + 0].ap()[0:128, 0:256],
                                 lhsT=XHT[:, dch, 128 * c:128 * (c + 1)],
                                 rhs=WPA[:, dch, 256:512],
                                 start=(dch == 0), stop=(dch == 7)), "te")
                for half in range(2):
                    for dch in range(8):
                        inc(t.matmul(P[(c % 2) * 3 + 1 + half].ap()[0:128, 0:512],
                                     lhsT=XHT[:, dch, 128 * c:128 * (c + 1)],
                                     rhs=WPA[:, dch, 512 + 512 * half:1024 + 512 * half],
                                     start=(dch == 0), stop=(dch == 7)), "te")
            te_tm = cnt["te"]

            @block.scalar
            def _(s, c=c, te_tm=te_tm):
                wat(s, "te", te_tm)
                inc(s.activation(out=KTM[:, c, :], in_=P[(c % 2) * 3 + 0].ap()[0:128, 0:256],
                                 func=AF.Copy), "ac")
                for half in range(2):
                    inc(s.activation(out=VTM[:, c, 512 * half:512 * (half + 1)],
                                     in_=P[(c % 2) * 3 + 1 + half].ap()[0:128, 0:512],
                                     func=AF.Copy), "ac")
            ac_tmv.append(cnt["ac"])
        ac_tm = cnt["ac"]

        if "q" in dbg_h:
            @block.gpsimd
            def _(g):
                wat(g, "ac", ac_q)
                inc(g.dma_start(out=dbg_h["q"].rearrange("(a p) t -> p a t", p=128), in_=QF[:]), "gd")
        if "k" in dbg_h:
            @block.gpsimd
            def _(g):
                wat(g, "ac", ac_k)
                inc(g.dma_start(out=dbg_h["k"].rearrange("(a p) t -> p a t", p=128), in_=KF[:]), "gd")
        if "v" in dbg_h:
            @block.gpsimd
            def _(g):
                wat(g, "ac", ac_tm)
                inc(g.dma_start(out=dbg_h["v"].rearrange("(c p) f -> p c f", p=128), in_=VTM[:]), "gd")

        if cut == "proj":
            finish()
            return nc

        sb_close("SACOL", "WSA", "WTMP", "WPA", "XHT")

        WOB = sb_open("WOB", [128, 8, D], BF16)
        WOTMP = sb_open("WOTMP", [128, D])

        prev_wo = dv_bcasts
        for ch in range(8):
            @block.gpsimd
            def _(g, ch=ch):
                wat(g, "dv", prev_wo)
                wat(g, "ac", ac_tm)
                inc(g.dma_start(out=WOTMP[:], in_=wo_h[128 * ch:128 * (ch + 1), :]), "gd")
            gd_wo = cnt["gd"]

            @block.vector
            def _(v, ch=ch, gd_wo=gd_wo):
                wat(v, "gd", gd_wo)
                inc(v.tensor_mul(out=WOB[:, ch, :], in0=WOTMP[:], in1=GAB[:]), "dv")
            prev_wo = cnt["dv"]
        dv_wob = cnt["dv"]

        @block.vector
        def _(v):
            wat(v, "dv", dv_wob)
            inc(v.engine_nop(), "dv")
        sb_close("WOTMP")

        if stage >= 2:
            QHG = sb_open("QHG", [16, 4, T], BF16)
            KHG = sb_open("KHG", [16, 4, T], BF16)
            KFA = sb_open("KFA", [128, 2, 512], BF16)
            KFB = sb_open("KFB", [128, 2, 512], BF16)
            KFL = sb_open("KFL", [128, 2, 64], BF16)
            QFA = sb_open("QFA", [128, 256], BF16)
            QFB = sb_open("QFB", [128, 256], BF16)
            UQ = sb_open("UQ", [128, 512], BF16)
            AM = sb_open("AM", [128, 512], BF16)
            SBF = sb_open("SBF", [128, 4 * 130], BF16)
            SC1B = sb_open("SC1B", [1, 4 * 65], BF16)
            SC2B = sb_open("SC2B", [16, 4 * 65], BF16)
            YRAW = sb_open("YRAW", [64, 4 * 256], BF16)
            YS2 = sb_open("YS2", [128, 2, 256], BF16)
            ZTM = sb_open("ZTM", [128, 64])
            ZFM = sb_open("ZFM", [64, 128])
            QE = sb_open("QE", [128, 256])
            SEL = sb_open("SEL", [64, 512])

            @block.vector
            def _(v):
                wat(v, "dv", dv_wob)
                inc(v.memset(SEL[:], 1.0), "dv")
            dv_sel = cnt["dv"]

            @block.gpsimd
            def _(g):
                wat(g, "dv", dv_sel)
                g.affine_select(out=SEL[:], in_=SEL[:],
                                pattern=[[-16, 2], [-32, 2], [-8, 2], [0, 64]],
                                compare_op=Alu.is_ge, fill=0.0, base=0,
                                channel_multiplier=1)
                g.affine_select(out=SEL[:], in_=SEL[:],
                                pattern=[[16, 2], [32, 2], [8, 2], [0, 64]],
                                compare_op=Alu.is_ge, fill=0.0, base=0,
                                channel_multiplier=-1)
                inc(g.engine_nop(), "gd", 1)
            gd_sel = cnt["gd"]

            @block.tensor
            def _(t):
                wat(t, "gd", gd_sel)

            L = {"u_ac": 0, "am_dv": 0, "qf_dv": 0, "y_ac": 0, "sbf_ac": 0,
                 "den_te": 0, "kf_te": 0, "z_done": 0, "ys_dv": dv_wob,
                 "p7_dv": dv_bcasts, "yraw_busy": 0, "hg_st": 0, "kf_dv": 0}

            for g4 in range(4):
                h0 = 4 * g4

                @block.tensor
                def _(t):
                    wat(t, "ac", cnt["ac"])
                    for bk in range(3):
                        inc(t.matmul(P[bk].ap()[0:128, 0:512], lhsT=ZROWB[0:1, 0:128],
                                     rhs=ONESBR[0:1, 0:512], start=True, stop=True), "te")

                @block.sync
                def _(sync):
                    wat(sync, "ac", ac_tm)
                    wat(sync, "dv", dv_wob)
                    wat(sync, "te", L["kf_te"])
                    for hl in range(4):
                        h = h0 + hl
                        fch, off = divmod(16 * h, 128)
                        inc(sync.dma_start(out=QHG[:, hl, :], in_=QF[off:off + 16, fch, :]), "st")
                        inc(sync.dma_start(out=KHG[:, hl, :], in_=KF[off:off + 16, fch, :]), "st")
                L["hg_st"] = cnt["st"]

                if g4 == 0 and "qh" in dbg_h:
                    @block.sync
                    def _(sync):
                        wat(sync, "st", L["hg_st"])
                        inc(sync.dma_start(out=dbg_h["qh"].rearrange("p (a t) -> p a t", a=4),
                                           in_=QHG[:]), "st")
                    L["hg_st"] = cnt["st"]

                for c in range(NC2):
                    @block.vector
                    def _(v):
                        wat(v, "ac", ac_tm)
                        wat(v, "te", L["kf_te"])
                        for kk in range(2):
                            tc = 2 * c + kk
                            base = KTM[:, tc, :]
                            for half, dst in ((0, KFA), (1, KFB)):
                                in0 = bass.AP(base.tensor, base.offset + 16 * h0 + 8 * half,
                                              [base.ap[0], [16, 4], [1, 8], [0, 16]])
                                in1 = bass.AP(base.tensor, base.offset + 16 * h0,
                                              [base.ap[0], [16, 4], [0, 8], [1, 16]])
                                v.tensor_tensor(
                                    out=dst[:, kk, :].rearrange("p (a b e) -> p a b e", a=4, b=8),
                                    in0=in0, in1=in1, op=Alu.mult)
                            v.tensor_scalar(out=KFL[:, kk, :],
                                            in0=base[:, 16 * h0:16 * h0 + 64],
                                            scalar1=BETA, scalar2=None, op0=Alu.mult)
                        inc(v.engine_nop(), "dv")
                    L["kf_dv"] = cnt["dv"]

                    for hl in range(4):
                        h = h0 + hl
                        q_sl = QHG[:, hl, 256 * c:256 * (c + 1)]

                        @block.tensor
                        def _(t):
                            wat(t, "ac", L["u_ac"])
                            wat(t, "st", L["hg_st"])
                            for kk in range(2):
                                inc(t.matmul(P[3].ap()[0:128, 256 * kk:256 * (kk + 1)],
                                             lhsT=KHG[:, hl, 256 * c + 128 * kk:
                                                      256 * c + 128 * (kk + 1)],
                                             rhs=q_sl, start=(kk == 0), stop=(kk == 1)), "te")
                        te_stil = cnt["te"]

                        @block.scalar
                        def _(s):
                            wat(s, "te", te_stil)
                            wat(s, "dv", L["am_dv"])
                            inc(s.activation(out=UQ[:], in_=P[3].ap()[0:128, 0:512],
                                             func=AF.Square, bias=SQB[:, 0:1]), "ac")
                        L["u_ac"] = cnt["ac"]

                        @block.vector
                        def _(v):
                            wat(v, "ac", L["u_ac"])
                            wat(v, "te", L["den_te"])
                            inc(v.scalar_tensor_tensor(out=AM[:], in0=UQ[:], scalar=0.5,
                                                       in1=MSTT[:], op0=Alu.add, op1=Alu.mult), "dv")
                        L["am_dv"] = cnt["dv"]

                        if g4 == 0 and c == 0 and hl == 0 and "am" in dbg_h:
                            @block.sync
                            def _(sync):
                                wat(sync, "dv", L["am_dv"])
                                inc(sync.dma_start(out=dbg_h["am"], in_=AM[:]), "st")
                                inc(sync.dma_start(out=dbg_h["uq"], in_=UQ[:]), "st")
                            L["hg_st"] = max(L["hg_st"], cnt["st"])

                        @block.tensor
                        def _(t):
                            wat(t, "ac", L["y_ac"])
                            wat(t, "dv", L["ys_dv"])
                            wat(t, "dv", L["qf_dv"])
                            inc(t.matmul(P[6].ap()[0:128, 0:256], lhsT=ETILE[:], rhs=q_sl,
                                         start=True, stop=True), "te")
                            inc(t.matmul(P[4].ap()[0:128, 0:256], lhsT=ESTRA[:], rhs=q_sl,
                                         start=True, stop=True), "te")
                            inc(t.matmul(P[4].ap()[0:128, 256:512], lhsT=ESTRB[:], rhs=q_sl,
                                         start=True, stop=True), "te")
                        te_qf = cnt["te"]

                        @block.scalar
                        def _(s):
                            wat(s, "te", te_qf)
                            inc(s.activation(out=QE[:], in_=P[6].ap()[0:128, 0:256],
                                             func=AF.Copy), "ac")
                        ac_qe = cnt["ac"]

                        @block.vector
                        def _(v):
                            wat(v, "te", te_qf)
                            wat(v, "ac", ac_qe)
                            inc(v.tensor_tensor(out=QFA[:], in0=P[4].ap()[0:128, 0:256],
                                                in1=QE[:], op=Alu.mult), "dv")
                            inc(v.tensor_tensor(out=QFB[:], in0=P[4].ap()[0:128, 256:512],
                                                in1=QE[:], op=Alu.mult), "dv")
                        L["qf_dv"] = cnt["dv"]

                        if g4 == 0 and c == 1 and hl == 0 and "qfa" in dbg_h:
                            @block.sync
                            def _(sync):
                                wat(sync, "dv", L["qf_dv"])
                                inc(sync.dma_start(out=dbg_h["qfa"], in_=QFA[:]), "st")
                                inc(sync.dma_start(out=dbg_h["qfb"], in_=QFB[:]), "st")
                            L["hg_st"] = max(L["hg_st"], cnt["st"])

                        @block.tensor
                        def _(t):
                            wat(t, "dv", L["qf_dv"])
                            wat(t, "dv", L["am_dv"])
                            if c > 0:
                                wat(t, "ac", L["sbf_ac"])
                            mms = []
                            for kk in range(2):
                                mms.append((VTM[:, 2 * c + kk, 64 * h:64 * (h + 1)],
                                            AM[:, 256 * kk:256 * (kk + 1)]))
                            if c > 0:
                                mms.append((SBF[:, 130 * hl:130 * hl + 64], QFA[:]))
                                mms.append((SBF[:, 130 * hl + 65:130 * hl + 129], QFB[:]))
                                mms.append((SC1B[0:1, 65 * hl:65 * hl + 64], ONESBR[0:1, 0:256]))
                                mms.append((SC2B[:, 65 * hl:65 * hl + 64], q_sl))
                            for i, (lh, rh) in enumerate(mms):
                                inc(t.matmul(P[4].ap()[0:64, 0:256], lhsT=lh, rhs=rh,
                                             start=(i == 0), stop=(i == len(mms) - 1)), "te")
                        te_y = cnt["te"]

                        @block.tensor
                        def _(t):
                            wat(t, "dv", L["z_done"])
                            if c > 0:
                                wat(t, "ac", L["sbf_ac"])
                            mms = []
                            for kk in range(2):
                                for qs in range(2):
                                    dc = 8 * (4 * qs + hl)
                                    mms.append((P[5].ap()[0:128, dc:dc + 1],
                                                AM[:, 256 * kk + 128 * qs:
                                                   256 * kk + 128 * (qs + 1)],
                                                ONESB[:]))
                            if c > 0:
                                for qs in range(2):
                                    q0 = 128 * qs
                                    dc = 8 * (4 * qs + hl)
                                    dstp = P[5].ap()[0:128, dc:dc + 1]
                                    mms.append((dstp, QFA[:, q0:q0 + 128],
                                                SBF[:, 130 * hl + 64:130 * hl + 65]))
                                    mms.append((dstp, QFB[:, q0:q0 + 128],
                                                SBF[:, 130 * hl + 129:130 * hl + 130]))
                                    mms.append((dstp, ONESBR[0:1, q0:q0 + 128],
                                                SC1B[0:1, 65 * hl + 64:65 * hl + 65]))
                                    mms.append((dstp, QHG[:, hl, 256 * c + q0:256 * c + q0 + 128],
                                                SC2B[:, 65 * hl + 64:65 * hl + 65]))
                            if hl == 0:
                                inc(t.matmul(P[5].ap()[0:128, 0:64],
                                             lhsT=ZROWB[0:1, 0:128],
                                             rhs=ONESBR[0:1, 0:64],
                                             start=True, stop=True), "te")
                            for i, (ou, lh, rh) in enumerate(mms):
                                inc(t.matmul(ou, lhsT=lh, rhs=rh,
                                             start=False, stop=True), "te")
                        L["den_te"] = cnt["te"]

                        if g4 == 0 and c == 0 and hl == 3 and "dd" in dbg_h:
                            @block.vector
                            def _(v):
                                wat(v, "te", L["den_te"])
                                inc(v.tensor_copy(out=EVS[:, 0:64],
                                                  in_=P[5].ap()[0:128, 0:64]), "dv")
                            dv_dd = cnt["dv"]

                            @block.sync
                            def _(sync, dv_dd=dv_dd):
                                wat(sync, "dv", dv_dd)
                                inc(sync.dma_start(out=dbg_h["dd"], in_=EVS[:, 0:64]), "st")

                        @block.scalar
                        def _(s):
                            wat(s, "te", te_y)
                            wat(s, "dv", L["yraw_busy"])
                            inc(s.activation(out=YRAW[:, 256 * hl:256 * (hl + 1)],
                                             in_=P[4].ap()[0:64, 0:256], func=AF.Copy), "ac")
                        L["y_ac"] = cnt["ac"]

                        te_state = cnt["te"]
                        if c < NC2 - 1:
                            @block.tensor
                            def _(t):
                                wat(t, "dv", L["kf_dv"])
                                if c > 0:
                                    wat(t, "ac", L["sbf_ac"])
                                for kk in range(2):
                                    fs = False
                                    vsl = VTM[:, 2 * c + kk, 64 * h:64 * (h + 1)]
                                    inc(t.matmul(P[0].ap()[0:128, 72 * hl:72 * hl + 64],
                                                 lhsT=KFA[:, kk, 128 * hl:128 * (hl + 1)],
                                                 rhs=vsl, start=False, stop=True), "te")
                                    inc(t.matmul(P[0].ap()[0:128, 72 * hl + 64:72 * hl + 65],
                                                 lhsT=KFA[:, kk, 128 * hl:128 * (hl + 1)],
                                                 rhs=ONESB[:], start=False, stop=True), "te")
                                    inc(t.matmul(P[1].ap()[0:128, 72 * hl:72 * hl + 64],
                                                 lhsT=KFB[:, kk, 128 * hl:128 * (hl + 1)],
                                                 rhs=vsl, start=False, stop=True), "te")
                                    inc(t.matmul(P[1].ap()[0:128, 72 * hl + 64:72 * hl + 65],
                                                 lhsT=KFB[:, kk, 128 * hl:128 * (hl + 1)],
                                                 rhs=ONESB[:], start=False, stop=True), "te")
                                    inc(t.matmul(P[2].ap()[32:33, 72 * hl:72 * hl + 64],
                                                 lhsT=ONESB[:], rhs=vsl, start=False, stop=True), "te")
                                    inc(t.matmul(P[2].ap()[32:33, 72 * hl + 64:72 * hl + 65],
                                                 lhsT=ONESB[:], rhs=ONESB[:], start=False, stop=True), "te")
                                    inc(t.matmul(P[2].ap()[0:16, 72 * hl:72 * hl + 64],
                                                 lhsT=KFL[:, kk, 16 * hl:16 * (hl + 1)],
                                                 rhs=vsl, start=False, stop=True), "te")
                                    inc(t.matmul(P[2].ap()[0:16, 72 * hl + 64:72 * hl + 65],
                                                 lhsT=KFL[:, kk, 16 * hl:16 * (hl + 1)],
                                                 rhs=ONESB[:], start=False, stop=True), "te")
                            te_state = cnt["te"]

                        if c < NC2 - 1:
                            @block.scalar
                            def _(s):
                                wat(s, "te", te_state)
                                inc(s.activation(out=SBF[:, 130 * hl:130 * hl + 65],
                                                 in_=P[0].ap()[0:128, 72 * hl:72 * hl + 65],
                                                 func=AF.Copy), "ac")
                                inc(s.activation(out=SBF[:, 130 * hl + 65:130 * hl + 130],
                                                 in_=P[1].ap()[0:128, 72 * hl:72 * hl + 65],
                                                 func=AF.Copy), "ac")
                                inc(s.activation(out=SC1B[0:1, 65 * hl:65 * (hl + 1)],
                                                 in_=P[2].ap()[32:33, 72 * hl:72 * hl + 65],
                                                 func=AF.Copy), "ac")
                                inc(s.activation(out=SC2B[:, 65 * hl:65 * (hl + 1)],
                                                 in_=P[2].ap()[0:16, 72 * hl:72 * hl + 65],
                                                 func=AF.Copy), "ac")
                            L["sbf_ac"] = cnt["ac"]

                            if g4 == 0 and c == 0 and hl == 3 and "stf" in dbg_h:
                                @block.sync
                                def _(sync):
                                    wat(sync, "ac", L["sbf_ac"])
                                    inc(sync.dma_start(out=dbg_h["stf"], in_=SBF[:]), "st")
                                    inc(sync.dma_start(out=dbg_h["st1"], in_=SC1B[:]), "st")
                                    inc(sync.dma_start(out=dbg_h["st2"], in_=SC2B[:]), "st")
                    L["kf_te"] = cnt["te"]

                    @block.vector
                    def _(v):
                        wat(v, "te", L["den_te"])
                        wat(v, "st", L.get("zzdump_st", 0))
                        inc(v.tensor_scalar(out=ZTM[:], in0=P[5].ap()[0:128, 0:64],
                                            scalar1=EPS_ATT, scalar2=None, op0=Alu.add), "dv")
                        vf(v)
                        inc(v.reciprocal(out=ZTM[:], in_=ZTM[:]), "dv")
                    dv_z = cnt["dv"]

                    if g4 == 0 and "zz" in dbg_h:
                        @block.sync
                        def _(sync, c=c):
                            wat(sync, "dv", dv_z)
                            inc(sync.dma_start(out=dbg_h["zz"][:, 64 * c:64 * (c + 1)],
                                               in_=ZTM[:]), "st")
                        L["zzdump_st"] = cnt["st"]

                    @block.tensor
                    def _(t):
                        wat(t, "dv", dv_z)
                        inc(t.transpose(P[5].ap()[0:64, 128:256], in_=ZTM[:], identity=IDENT[:]), "te")
                    te_zt = cnt["te"]

                    @block.vector
                    def _(v):
                        wat(v, "te", te_zt)
                        inc(v.tensor_copy(out=ZFM[:], in_=P[5].ap()[0:64, 128:256]), "dv")
                    L["z_done"] = cnt["dv"]

                    @block.tensor
                    def _(t):
                        wat(t, "dv", L["z_done"])
                        wat(t, "dv", L["qf_dv"])
                        for pr in range(2):
                            for qs in range(2):
                                inc(t.matmul(P[6].ap()[0:128, 256 * pr + 128 * qs:
                                                       256 * pr + 128 * (qs + 1)],
                                             lhsT=SEL[:, 256 * pr + 128 * qs:
                                                      256 * pr + 128 * (qs + 1)],
                                             rhs=ZFM[:],
                                             start=True, stop=True), "te")
                    te_zb = cnt["te"]

                    if g4 == 0 and "yr" in dbg_h:
                        @block.sync
                        def _(sync, c=c):
                            wat(sync, "ac", L["y_ac"])
                            inc(sync.dma_start(out=dbg_h["yr"][:, 1024 * c:1024 * (c + 1)],
                                               in_=YRAW[:]), "st")

                    @block.vector
                    def _(v):
                        wat(v, "te", te_zb)
                        wat(v, "ac", L["y_ac"])
                        wat(v, "st", L.get("ysdump_st", 0))
                        for hl in range(4):
                            pr, half = divmod(hl, 2)
                            inc(v.tensor_tensor(out=YS2[64 * half:64 * half + 64, pr, :],
                                                in0=YRAW[:, 256 * hl:256 * (hl + 1)],
                                                in1=P[6].ap()[64 * half:64 * half + 64,
                                                              256 * pr:256 * (pr + 1)],
                                                op=Alu.mult), "dv")
                    L["ys_dv"] = cnt["dv"]
                    L["yraw_busy"] = cnt["dv"]

                    if g4 == 3 and c == 0 and "ys" in dbg_h:
                        @block.sync
                        def _(sync):
                            wat(sync, "dv", L["ys_dv"])
                            inc(sync.dma_start(out=dbg_h["ys"].rearrange("p (a q) -> p a q", a=2),
                                               in_=YS2[:]), "st")
                        L["ysdump_st"] = cnt["st"]

                    for qs in range(2):
                        for dh in range(2):
                            @block.tensor
                            def _(t):
                                wat(t, "dv", L["ys_dv"])
                                wat(t, "dv", L["p7_dv"])
                                for pr in range(2):
                                    inc(t.matmul(P[7].ap()[0:128, 0:512],
                                                 lhsT=YS2[:, pr, 128 * qs:128 * (qs + 1)],
                                                 rhs=WOB[:, 2 * g4 + pr, 512 * dh:512 * (dh + 1)],
                                                 start=(pr == 0), stop=(pr == 1)), "te")
                            te_p7 = cnt["te"]

                            @block.vector
                            def _(v):
                                wat(v, "te", te_p7)
                                tc = 2 * c + qs
                                inc(v.tensor_add(out=XT[:, tc, 512 * dh:512 * (dh + 1)],
                                                 in0=P[7].ap()[0:128, 0:512],
                                                 in1=XT[:, tc, 512 * dh:512 * (dh + 1)]), "dv")
                            L["p7_dv"] = cnt["dv"]

            sb_close("SEL", "QE", "ZFM", "ZTM", "YS2", "YRAW", "SC2B", "SC1B", "SBF", "AM",
                     "UQ", "QFB", "QFA", "KFL", "KFB", "KFA", "KHG", "QHG")

        if "x2" in dbg_h:
            @block.sync
            def _(sync):
                wt(sync, "dv")
                inc(sync.dma_start(out=dbg_h["x2"].rearrange("(c p) d -> p c d", p=128),
                                   in_=XT[:]), "st")

        if stage < 3:
            @block.sync
            def _(sync):
                wt(sync, "dv", "ac", "te", "gd", "ld")
                inc(sync.dma_start(out=out_h.rearrange("(c p) d -> p c d", p=128), in_=XT[:]), "st")
                wt(sync, "st")
            for n in reversed(list(live.keys())):
                sb_close(n)
            return nc

        sb_close("WOB", "VTM", "KTM", "KF", "QF")
        H2C = sb_open("H2C", [128, D])

        @block.vector
        def _(v):
            wt(v, "dv")
            wt(v, "ac")
            wt(v, "te")
            for c in range(NT):
                v.tensor_scalar(out=H2C[:], in0=XT[:, c, :], scalar1=1.0,
                                scalar2=0.0, op0=Alu.mult, op1=Alu.add,
                                accum_out=STATM[:, c:c + 1])
                v.scalar_tensor_tensor(out=H2C[:], in0=XT[:, c, :], scalar=1.0,
                                       in1=XT[:, c, :], op0=Alu.mult, op1=Alu.mult,
                                       accum_out=STTMP[:, c:c + 1])
            vf(v)
            v.tensor_scalar(out=STATM[:], in0=STATM[:], scalar1=1.0 / D, scalar2=None,
                            op0=Alu.mult)
            v.tensor_scalar(out=STTMP[:], in0=STTMP[:], scalar1=1.0 / D, scalar2=EPS_LN,
                            op0=Alu.mult, op1=Alu.add)
            vf(v)
            v.tensor_mul(out=BNS[:, 0:NT], in0=STATM[:], in1=STATM[:])
            vf(v)
            v.tensor_tensor(out=STTMP[:], in0=STTMP[:], in1=BNS[:, 0:NT],
                            op=Alu.subtract)
            vf(v)
            v.reciprocal(out=STATR[:], in_=STTMP[:])
            inc(v.engine_nop(), "dv")
        dv_bn2 = cnt["dv"]

        @block.scalar
        def _(s):
            wat(s, "dv", dv_bn2)
            inc(s.activation(out=STATR[:], in_=STATR[:], func=AF.Sqrt), "ac")
        ac_sq2 = cnt["ac"]

        @block.vector
        def _(v):
            wat(v, "ac", ac_sq2)
            v.tensor_mul(out=BNS[:, 0:NT], in0=STATR[:], in1=STATR[:])
            vf(v)
            v.tensor_mul(out=BNS[:, 0:NT], in0=BNS[:, 0:NT], in1=STTMP[:])
            vf(v)
            v.tensor_scalar(out=BNS[:, 0:NT], in0=BNS[:, 0:NT], scalar1=-0.5,
                            scalar2=1.5, op0=Alu.mult, op1=Alu.add)
            vf(v)
            v.tensor_mul(out=STATR[:], in0=STATR[:], in1=BNS[:, 0:NT])
            vf(v)
            for c in range(NT):
                v.tensor_scalar(out=H2C[:], in0=XT[:, c, :],
                                scalar1=STATM[:, c:c + 1], scalar2=STATR[:, c:c + 1],
                                op0=Alu.subtract, op1=Alu.mult)
                v.tensor_mul(out=H2C[:], in0=H2C[:], in1=CB1B[:])
                v.tensor_add(out=H2C[:], in0=H2C[:], in1=SBB[:])
                v.reduce_max(out=GMX[:, c:c + 1], in_=H2C[:], axis=AX.X,
                             apply_absolute_value=True)
            v.reduce_max(out=GMX[:, 30:31], in_=GMX[:, 0:NT], axis=AX.X)
            inc(v.engine_nop(), "dv")
        dv_h2 = cnt["dv"]

        @block.sync
        def _(sync):
            wat(sync, "dv", dv_h2)
            inc(sync.dma_start(out=scr_x2.rearrange("(c p) d -> p c d", p=128), in_=XT[:]), "st")
            wt(sync, "st")
        st_x2 = cnt["st"]

        sb_close("H2C", "XT")

        GBSC3 = GBSC_tile = None
        GBSC = sb_open("GBSC", [128, D])
        SWDR = sb_open("SWDR", [1, D])
        XRB = sb_open("XRB", [128, 2, D])
        HT2 = sb_open("HT2", [128, D])
        XQT = sb_open("XQT", [128, NT, D], BF16)

        @block.tensor
        def _(t):
            wat(t, "dv", dv_h2)
            inc(t.transpose(P[5].ap()[0:1, 0:128], in_=GMX[:, 30:31], identity=IDENT[:]), "te")
        te_mx = cnt["te"]

        @block.vector
        def _(v):
            wat(v, "te", te_mx)
            inc(v.tensor_copy(out=CCS[:], in_=P[5].ap()[0:1, 0:128]), "dv")
            vf(v)
            inc(v.reduce_max(out=SCR[0:1, 2:3], in_=CCS[:], axis=AX.X), "dv")
            vf(v)
            inc(v.tensor_scalar(out=CCS[:], in0=ONES1[0:1, 0:128], scalar1=SCR[0:1, 2:3],
                                scalar2=None, op0=Alu.mult), "dv")
        dv_mx = cnt["dv"]

        @block.sync
        def _(sync):
            wat(sync, "dv", dv_mx)
            inc(sync.dma_start(out=cc_m1_in, in_=CCS[:]), "st")
        st_m1 = cnt["st"]

        @block.gpsimd
        def _(g):
            wat(g, "st", st_m1)
            inc(g.collective_compute("AllReduce", Alu.max,
                                     replica_groups=[list(range(8))],
                                     ins=[cc_m1_in.opt()], outs=[cc_m1_out.opt()]), "cc", 1)
            wat(g, "cc", cnt["cc"])
            inc(g.dma_start(out=CCS[:], in_=cc_m1_out), "gd")
        gd_m1 = cnt["gd"]

        @block.scalar
        def _(s):
            wat(s, "ld", ld_in)
            inc(s.activation(out=SCR[0:1, 4:5], in_=SCR[0:1, 0:1], func=AF.Sigmoid), "ac")
        ac_sig = cnt["ac"]

        @block.vector
        def _(v):
            wat(v, "gd", gd_m1)
            wat(v, "ac", ac_sig)
            v.scalar_tensor_tensor(out=SCR[0:1, 5:6], in0=SCR[0:1, 4:5],
                                   scalar=1.0 / QMAX, in1=CCS[0:1, 0:1],
                                   op0=Alu.mult, op1=Alu.mult)
            vf(v)
            v.reciprocal(out=SCR[0:1, 6:7], in_=SCR[0:1, 5:6])
            inc(v.engine_nop(), "dv")
        dv_sx = cnt["dv"]

        @block.tensor
        def _(t):
            wat(t, "dv", dv_sx)
            inc(t.matmul(P[5].ap()[0:128, 0:1], lhsT=ONES1[0:1, 0:128], rhs=SCR[0:1, 6:7],
                         start=True, stop=True), "te")
            inc(t.matmul(P[5].ap()[0:128, 8:9], lhsT=ONES1[0:1, 0:128], rhs=SCR[0:1, 5:6],
                         start=True, stop=True), "te")
        te_sx = cnt["te"]

        @block.vector
        def _(v):
            wat(v, "te", te_sx)
            v.tensor_copy(out=SCL[:, 0:1], in_=P[5].ap()[0:128, 0:1])
            inc(v.tensor_copy(out=SCL[:, 1:2], in_=P[5].ap()[0:128, 8:9]), "dv")
        dv_sxb = cnt["dv"]

        prev_q = 0
        for c in range(NT):
            @block.gpsimd
            def _(g, c=c):
                wat(g, "st", st_x2)
                if c >= 2:
                    wat(g, "dv", prev_q)
                inc(g.dma_start(out=XRB[:, c % 2, :],
                                in_=scr_x2[128 * c:128 * (c + 1), :]), "gd")
            gd_xr = cnt["gd"]

            @block.vector
            def _(v, c=c, gd_xr=gd_xr):
                wat(v, "gd", gd_xr)
                wat(v, "dv", dv_sxb)
                v.tensor_scalar(out=HT2[:], in0=XRB[:, c % 2, :],
                                scalar1=STATM[:, c:c + 1], scalar2=STATR[:, c:c + 1],
                                op0=Alu.subtract, op1=Alu.mult)
                v.tensor_mul(out=HT2[:], in0=HT2[:], in1=CB1B[:])
                v.tensor_add(out=HT2[:], in0=HT2[:], in1=SBB[:])
                v.tensor_scalar(out=HT2[:], in0=HT2[:], scalar1=SCL[:, 0:1],
                                scalar2=MAGIC, op0=Alu.mult, op1=Alu.add)
                v.tensor_scalar(out=HT2[:], in0=HT2[:], scalar1=MAGIC,
                                scalar2=-128.0, op0=Alu.subtract, op1=Alu.max)
                inc(v.tensor_scalar(out=XQT[:, c, :], in0=HT2[:], scalar1=127.0,
                                    scalar2=None, op0=Alu.min), "dv")
            prev_q = cnt["dv"]
        dv_xq = cnt["dv"]

        @block.sync
        def _(sync):
            wat(sync, "dv", dv_xq)
            inc(sync.dma_start(out=scr_xq.rearrange("(c p) d -> p c d", p=128), in_=XQT[:]), "st")
            wt(sync, "st")
        st_xq = cnt["st"]

        sb_close("XQT")
        GBUF = sb_open("GBUF", [128, 2, T])
        XQF = sb_open("XQF", [128, 8, T], BF16)
        WQTMP = sb_open("WQTMP", [128, 2, NI])
        WQI = sb_open("WQI", [128, 2, NI], BF16)

        @block.sync
        def _(sync):
            for dch in range(8):
                inc(sync.dma_start(out=XQF[:, dch, :],
                                   in_=scr_xq[:, 128 * dch:128 * (dch + 1)], transpose=True), "ld")
        ld_xqf = cnt["ld"]

        qjobs = []
        for ch in range(NIC):
            qjobs.append((wg_h, scr_wg, 0, ch, D))
        for ch in range(NIC):
            qjobs.append((wu_h, scr_wu, 22, ch, D))
        for ch in range(8):
            qjobs.append((wd_h, scr_wd, 44, ch, NI))

        qv_done = [0, 0]
        qst_done = [0, 0]
        for jj, (w_h, scr, wsc_off, ch, width) in enumerate(qjobs):
            pb = jj % 2
            @block.gpsimd
            def _(g, w_h=w_h, ch=ch, width=width, pb=pb, prev=qv_done[pb]):
                wat(g, "dv", prev)
                wat(g, "st", st_xq)
                inc(g.dma_start(out=WQTMP[:, pb, 0:width],
                                in_=w_h[128 * ch:128 * (ch + 1), :]), "gd")
            gd_l = cnt["gd"]

            @block.vector
            def _(v, wsc_off=wsc_off, ch=ch, width=width, pb=pb, gd_l=gd_l,
                  prevst=qst_done[pb]):
                wat(v, "gd", gd_l)
                wat(v, "st", prevst)
                v.reduce_max(out=WSC[:, wsc_off + ch:wsc_off + ch + 1],
                             in_=WQTMP[:, pb, 0:width], axis=AX.X,
                             apply_absolute_value=True)
                vf(v)
                v.tensor_scalar(out=WSC[:, wsc_off + ch:wsc_off + ch + 1],
                                in0=WSC[:, wsc_off + ch:wsc_off + ch + 1],
                                scalar1=1.0 / QMAX, scalar2=None, op0=Alu.mult)
                vf(v)
                v.reciprocal(out=EVS[:, 0:1],
                             in_=WSC[:, wsc_off + ch:wsc_off + ch + 1])
                vf(v)
                v.tensor_scalar(out=WQTMP[:, pb, 0:width], in0=WQTMP[:, pb, 0:width],
                                scalar1=EVS[:, 0:1], scalar2=MAGIC,
                                op0=Alu.mult, op1=Alu.add)
                v.tensor_scalar(out=WQTMP[:, pb, 0:width], in0=WQTMP[:, pb, 0:width],
                                scalar1=MAGIC, scalar2=-128.0,
                                op0=Alu.subtract, op1=Alu.max)
                inc(v.tensor_scalar(out=WQI[:, pb, 0:width], in0=WQTMP[:, pb, 0:width],
                                    scalar1=127.0, scalar2=None, op0=Alu.min), "dv")
            qv_done[pb] = cnt["dv"]

            @block.sync
            def _(sync, scr=scr, ch=ch, width=width, pb=pb, dvq=qv_done[pb]):
                wat(sync, "dv", dvq)
                inc(sync.dma_start(out=scr[128 * ch:128 * (ch + 1), :],
                                   in_=WQI[:, pb, 0:width]), "st")
            qst_done[pb] = cnt["st"]

        @block.gpsimd
        def _(g):
            wat(g, "st", cnt["st"])
            inc(g.dma_start(out=SCR[0:1, 16:17], in_=clipx_h), "gd")
        gd_wq = cnt["gd"]

        for ch in range(8):
            @block.tensor
            def _(t, ch=ch):
                wat(t, "dv", cnt["dv"])
                inc(t.transpose(P[5].ap()[0:1, 0:128], in_=WSC[:, 44 + ch:44 + ch + 1],
                                identity=IDENT[:]), "te")
            te_w = cnt["te"]

            @block.vector
            def _(v, ch=ch, te_w=te_w):
                wat(v, "te", te_w)
                inc(v.tensor_copy(out=SWDR[0:1, 128 * ch:128 * (ch + 1)],
                                  in_=P[5].ap()[0:1, 0:128]), "dv")
        dv_swd = cnt["dv"]

        sb_close("WQI", "WQTMP")
        WGTS = sb_open("WGTS", [128, 2, 8, 512], BF16)
        WUTS = sb_open("WUTS", [128, 2, 8, 512], BF16)
        SIG = sb_open("SIG", [128, T])
        GRE = sb_open("GRE", [128, T])
        URE = sb_open("URE", [128, T])
        ALP = sb_open("ALP", [128, 2])

        gu_blocks = [(0, 512), (512, 512), (1024, 512), (1536, 512), (2048, 512),
                     (2560, 256)]
        gb_gd = [0, 0]
        last_g_dv = [0]
        gu_evac = [0, 0]
        buf_free_ac = [0, 0]
        blk_ld = []
        for bi, (i0, width) in enumerate(gu_blocks):
            pb = bi % 2
            @block.sync
            def _(sync, bi=bi, pb=pb, i0=i0, width=width):
                wat(sync, "gd", gd_wq)
                wat(sync, "dv", dv_swd)
                wat(sync, "ac", buf_free_ac[pb])
                for dch in range(8):
                    inc(sync.dma_start(out=WGTS[:, pb, dch, 0:width],
                                       in_=scr_wg[i0:i0 + width, 128 * dch:128 * (dch + 1)],
                                       transpose=True), "ld")
                    inc(sync.dma_start(out=WUTS[:, pb, dch, 0:width],
                                       in_=scr_wu[i0:i0 + width, 128 * dch:128 * (dch + 1)],
                                       transpose=True), "ld")
            blk_ld.append(cnt["ld"])

            for sub in range(width // 128):
                ic = i0 // 128 + sub
                @block.vector
                def _(v, ic=ic):
                    wat(v, "dv", dv_sxb)
                    wat(v, "ac", cnt["ac"])
                    wat(v, "gd", cnt["gd"])
                    v.tensor_mul(out=ALP[:, 0:1], in0=WSC[:, ic:ic + 1], in1=SCL[:, 1:2])
                    v.tensor_mul(out=ALP[:, 1:2], in0=WSC[:, 22 + ic:22 + ic + 1], in1=SCL[:, 1:2])
                    inc(v.engine_nop(), "dv")
                dv_alp = cnt["dv"]

                for half in range(2):
                    @block.tensor
                    def _(t, ic=ic, pb=pb, sub=sub, bi=bi, half=half,
                          pe=gu_evac[half]):
                        wat(t, "ld", ld_xqf)
                        wat(t, "ld", blk_ld[bi])
                        wat(t, "dv", dv_sxb)
                        wat(t, "ac", pe)
                        for j in (2 * half, 2 * half + 1):
                            for dch in range(8):
                                inc(t.matmul(P[j].ap()[0:128, 0:512],
                                             lhsT=WGTS[:, pb, dch, 128 * sub:128 * (sub + 1)],
                                             rhs=XQF[:, dch, 512 * j:512 * (j + 1)],
                                             start=(dch == 0), stop=(dch == 7)), "te")
                            for dch in range(8):
                                inc(t.matmul(P[4 + j].ap()[0:128, 0:512],
                                             lhsT=WUTS[:, pb, dch, 128 * sub:128 * (sub + 1)],
                                             rhs=XQF[:, dch, 512 * j:512 * (j + 1)],
                                             start=(dch == 0), stop=(dch == 7)), "te")
                    te_h = cnt["te"]

                    @block.scalar
                    def _(s, te_h=te_h, dv_alp=dv_alp, pg=last_g_dv[0], half=half):
                        wat(s, "te", te_h)
                        wat(s, "dv", dv_alp)
                        wat(s, "dv", pg)
                        for j in (2 * half, 2 * half + 1):
                            inc(s.activation(out=SIG[:, 512 * j:512 * (j + 1)],
                                             in_=P[j].ap()[0:128, 0:512], func=AF.Sigmoid,
                                             scale=ALP[:, 0:1]), "ac")
                            inc(s.activation(out=GRE[:, 512 * j:512 * (j + 1)],
                                             in_=P[j].ap()[0:128, 0:512], func=AF.Copy,
                                             scale=ALP[:, 0:1]), "ac")
                            inc(s.activation(out=URE[:, 512 * j:512 * (j + 1)],
                                             in_=P[4 + j].ap()[0:128, 0:512], func=AF.Copy,
                                             scale=ALP[:, 1:2]), "ac")
                    gu_evac[half] = cnt["ac"]
                ac_gu = cnt["ac"]

                @block.vector
                def _(v, ic=ic, ac_gu=ac_gu):
                    wat(v, "ac", ac_gu)
                    wat(v, "gd", gb_gd[ic % 2])
                    v.tensor_mul(out=GBUF[:, ic % 2, :], in0=GRE[:], in1=SIG[:])
                    v.tensor_mul(out=GBUF[:, ic % 2, :], in0=GBUF[:, ic % 2, :], in1=URE[:])
                    v.reduce_max(out=GMX[:, ic:ic + 1], in_=GBUF[:, ic % 2, :], axis=AX.X,
                                 apply_absolute_value=True)
                    inc(v.engine_nop(), "dv")
                dv_g = cnt["dv"]
                last_g_dv[0] = dv_g

                @block.gpsimd
                def _(g, ic=ic, dv_g=dv_g):
                    wat(g, "dv", dv_g)
                    inc(g.dma_start(out=scr_g[128 * ic:128 * (ic + 1), :],
                                    in_=GBUF[:, ic % 2, :]), "gd")
                gb_gd[ic % 2] = cnt["gd"]
            buf_free_ac[pb] = cnt["ac"]
        gd_g = cnt["gd"]

        @block.vector
        def _(v):
            wt(v, "dv")
            inc(v.reduce_max(out=GMX[:, 30:31], in_=GMX[:, 0:NIC], axis=AX.X), "dv")
        dv_gm = cnt["dv"]

        @block.tensor
        def _(t):
            wat(t, "dv", dv_gm)
            inc(t.transpose(P[5].ap()[0:1, 0:128], in_=GMX[:, 30:31], identity=IDENT[:]), "te")
        te_gm = cnt["te"]

        @block.vector
        def _(v):
            wat(v, "te", te_gm)
            inc(v.tensor_copy(out=CCS[:], in_=P[5].ap()[0:1, 0:128]), "dv")
            vf(v)
            inc(v.reduce_max(out=SCR[0:1, 8:9], in_=CCS[:], axis=AX.X), "dv")
            vf(v)
            inc(v.tensor_scalar(out=CCS[:], in0=ONES1[0:1, 0:128], scalar1=SCR[0:1, 8:9],
                                scalar2=None, op0=Alu.mult), "dv")
        dv_gm2 = cnt["dv"]

        @block.sync
        def _(sync):
            wat(sync, "dv", dv_gm2)
            inc(sync.dma_start(out=cc_m2_in, in_=CCS[:]), "st")
        st_m2 = cnt["st"]

        @block.gpsimd
        def _(g):
            wat(g, "st", st_m2)
            inc(g.collective_compute("AllReduce", Alu.max,
                                     replica_groups=[list(range(8))],
                                     ins=[cc_m2_in.opt()], outs=[cc_m2_out.opt()]), "cc", 1)
            wat(g, "cc", cnt["cc"])
            inc(g.dma_start(out=CCS[:], in_=cc_m2_out), "gd")
        gd_m2 = cnt["gd"]

        @block.scalar
        def _(s):
            wat(s, "ld", ld_in)
            inc(s.activation(out=SCR[0:1, 9:10], in_=SCR[0:1, 1:2], func=AF.Sigmoid), "ac")
        ac_sig2 = cnt["ac"]

        @block.vector
        def _(v):
            wat(v, "gd", gd_m2)
            wat(v, "ac", ac_sig2)
            v.scalar_tensor_tensor(out=SCR[0:1, 10:11], in0=SCR[0:1, 9:10],
                                   scalar=1.0 / QMAX, in1=CCS[0:1, 0:1],
                                   op0=Alu.mult, op1=Alu.mult)
            vf(v)
            v.reciprocal(out=SCR[0:1, 11:12], in_=SCR[0:1, 10:11])
            inc(v.engine_nop(), "dv")
        dv_sd = cnt["dv"]

        @block.tensor
        def _(t):
            wat(t, "dv", dv_sd)
            inc(t.matmul(P[5].ap()[0:128, 0:1], lhsT=ONES1[0:1, 0:128], rhs=SCR[0:1, 11:12],
                         start=True, stop=True), "te")
            inc(t.matmul(P[5].ap()[0:128, 8:9], lhsT=ONES1[0:1, 0:128], rhs=SCR[0:1, 10:11],
                         start=True, stop=True), "te")
        te_sd = cnt["te"]

        @block.vector
        def _(v):
            wat(v, "te", te_sd)
            v.tensor_copy(out=SCL[:, 2:3], in_=P[5].ap()[0:128, 0:1])
            inc(v.tensor_copy(out=SCL[:, 3:4], in_=P[5].ap()[0:128, 8:9]), "dv")
        dv_sdb = cnt["dv"]

        te_gu_all = cnt["te"]
        sb_close("ALP", "URE", "GRE", "SIG", "WUTS", "WGTS", "XQF")
        HQF = sb_open("HQF", [128, NIC, T], BF16)
        WDT = sb_open("WDT", [128, NIC, D], BF16)
        EVO = sb_open("EVO", [128, 2, D])

        @block.sync
        def _(sync):
            wat(sync, "gd", gd_wq)
            wat(sync, "te", te_gu_all)
            wat(sync, "dv", dv_sdb)
            for ic in range(NIC):
                inc(sync.dma_start(out=WDT[:, ic, :], in_=scr_wd[:, 128 * ic:128 * (ic + 1)],
                                   transpose=True), "ld")
        ld_wdt = cnt["ld"]

        for jj in range(2):
            @block.tensor
            def _(t, jj=jj):
                wat(t, "dv", dv_swd)
                wat(t, "dv", cnt["dv"])
                inc(t.matmul(P[7].ap()[0:128, 0:512], lhsT=ONES1[0:1, 0:128],
                             rhs=SWDR[0:1, 512 * jj:512 * (jj + 1)],
                             start=True, stop=True), "te")
            te_j = cnt["te"]

            @block.vector
            def _(v, jj=jj, te_j=te_j):
                wat(v, "te", te_j)
                wat(v, "dv", dv_sdb)
                v.tensor_tensor(out=GBSC[:, 512 * jj:512 * (jj + 1)],
                                in0=GBB[:, 512 * jj:512 * (jj + 1)],
                                in1=P[7].ap()[0:128, 0:512],
                                op=Alu.mult)
                inc(v.tensor_scalar(out=GBSC[:, 512 * jj:512 * (jj + 1)],
                                    in0=GBSC[:, 512 * jj:512 * (jj + 1)],
                                    scalar1=SCL[:, 3:4], scalar2=None, op0=Alu.mult), "dv")
        dv_gbsc = cnt["dv"]

        for ic in range(NIC):
            @block.gpsimd
            def _(g, ic=ic):
                wat(g, "gd", gd_g)
                wat(g, "dv", cnt["dv"])
                inc(g.dma_start(out=GBUF[:, ic % 2, :],
                                in_=scr_g[128 * ic:128 * (ic + 1), :]), "gd")
            gd_r = cnt["gd"]

            @block.vector
            def _(v, ic=ic, gd_r=gd_r):
                wat(v, "gd", gd_r)
                wat(v, "dv", dv_sdb)
                v.tensor_scalar(out=GBUF[:, ic % 2, :], in0=GBUF[:, ic % 2, :],
                                scalar1=SCL[:, 2:3],
                                scalar2=MAGIC, op0=Alu.mult, op1=Alu.add)
                v.tensor_scalar(out=GBUF[:, ic % 2, :], in0=GBUF[:, ic % 2, :],
                                scalar1=MAGIC,
                                scalar2=-128.0, op0=Alu.subtract, op1=Alu.max)
                inc(v.tensor_scalar(out=HQF[:, ic, :], in0=GBUF[:, ic % 2, :],
                                    scalar1=127.0,
                                    scalar2=None, op0=Alu.min), "dv")
        dv_hq = cnt["dv"]

        prev_ev = dv_gbsc
        st_tiles = [0] * NT
        for tc in range(NT):
            @block.gpsimd
            def _(g, tc=tc):
                wat(g, "st", st_x2)
                wat(g, "dv", cnt["dv"])
                inc(g.dma_start(out=XRB[:, tc % 2, :],
                                in_=scr_x2[128 * tc:128 * (tc + 1), :]), "gd")
            gd_x2 = cnt["gd"]

            @block.tensor
            def _(t, tc=tc):
                wat(t, "dv", dv_hq)
                wat(t, "ld", ld_wdt)
                wat(t, "dv", prev_ev)
                for dh in range(2):
                    for ic in range(NIC):
                        inc(t.matmul(P[2 * (tc % 2) + dh].ap()[0:128, 0:512],
                                     lhsT=HQF[:, ic, 128 * tc:128 * (tc + 1)],
                                     rhs=WDT[:, ic, 512 * dh:512 * (dh + 1)],
                                     start=(ic == 0), stop=(ic == NIC - 1)), "te")
            te_dn = cnt["te"]

            @block.vector
            def _(v, tc=tc, te_dn=te_dn, gd_x2=gd_x2):
                wat(v, "te", te_dn)
                wat(v, "gd", gd_x2)
                wat(v, "dv", dv_gbsc)
                if tc >= 2:
                    wat(v, "st", st_tiles[tc - 2])
                for dh in range(2):
                    v.tensor_tensor(out=EVO[:, tc % 2, 512 * dh:512 * (dh + 1)],
                                    in0=P[2 * (tc % 2) + dh].ap()[0:128, 0:512],
                                    in1=GBSC[:, 512 * dh:512 * (dh + 1)], op=Alu.mult)
                inc(v.tensor_add(out=EVO[:, tc % 2, :], in0=EVO[:, tc % 2, :],
                                 in1=XRB[:, tc % 2, :]), "dv")
            prev_ev = cnt["dv"]

            @block.sync
            def _(sync, tc=tc, prev_ev=prev_ev):
                wat(sync, "dv", prev_ev)
                inc(sync.dma_start(out=out_h[128 * tc:128 * (tc + 1), :],
                                   in_=EVO[:, tc % 2, :]), "st")
            st_tiles[tc] = cnt["st"]

        @block.sync
        def _(sync):
            wt(sync, "st", "dv", "ac", "te", "gd", "ld")

        for n in reversed(list(live.keys())):
            sb_close(n)

    return nc


def kernel(L0_attn_hidden_states, L0_attn_Wq, L0_attn_Wk, L0_attn_Wv, L0_attn_Wo,
           L0_ffn_clip_x, L0_ffn_clip_down, L0_ffn_w_gate, L0_ffn_w_up,
           L0_ffn_w_down, L0_c, L0_wmod):
    x = np.ascontiguousarray(L0_attn_hidden_states, dtype=np.float32)
    nc = build_nc(DBG, STAGE, CUT)
    base = {
        "wq": np.ascontiguousarray(L0_attn_Wq, np.float32),
        "wk": np.ascontiguousarray(L0_attn_Wk, np.float32),
        "wv": np.ascontiguousarray(L0_attn_Wv, np.float32),
        "wo": np.ascontiguousarray(L0_attn_Wo, np.float32),
        "wg": np.ascontiguousarray(L0_ffn_w_gate, np.float32),
        "wu": np.ascontiguousarray(L0_ffn_w_up, np.float32),
        "wd": np.ascontiguousarray(L0_ffn_w_down, np.float32),
        "cvec": np.ascontiguousarray(L0_c, np.float32),
        "clipx": np.asarray(L0_ffn_clip_x, np.float32).reshape(1, 1),
        "clipd": np.asarray(L0_ffn_clip_down, np.float32).reshape(1, 1),
    }
    wmod = np.ascontiguousarray(L0_wmod, np.float32)
    in_maps = []
    for b in range(8):
        m = dict(base)
        m["x"] = np.ascontiguousarray(x[b])
        m["wmod"] = np.ascontiguousarray(wmod[128 * b:128 * (b + 1)])
        in_maps.append(m)
    res = run_bass_kernel_spmd(nc, in_maps, core_ids=list(range(8)))
    out = np.stack([res.results[b]["out"] for b in range(8)], axis=0)
    globals()["_last_results"] = res
    return out
